# version 24
# baseline (speedup 1.0000x reference)
import contextlib
import numpy as np
import ml_dtypes
import concourse.bass as bass
import concourse.mybir as mybir
from concourse.bass_utils import run_bass_kernel_spmd

F32 = mybir.dt.float32
BF16 = mybir.dt.bfloat16
I32 = mybir.dt.int32
AF = mybir.ActivationFunctionType
ALU = mybir.AluOpType
AX = mybir.AxisListType

SEQ = 8192
D = 2048
AQK = 1536
NQ = 2048
N_HT = 16
N_AT = 12
N_IT = 64
N_PAIR = 12
NSLOT = 4
MARGIN = 30.0
DEBUG_TAPS = False
GU_ENABLE = True


def qkv_ht_cols(ht):
    if ht < 2:
        return 0, 1536
    if ht < 8:
        return 256, 1280
    return 768, 768


def qkv_at_hts(at):
    if at < 2:
        return range(0, 2)
    if at < 6:
        return range(0, 8)
    return range(0, 16)


def wo_ht_ats(ht):
    if ht < 2:
        return range(0, 2)
    if ht < 8:
        return range(0, 6)
    return range(0, 12)


V_BANK_PLAN = [
    [(0, 512, [1]), (256, 256, list(range(2, 8)))],
    [(512, 512, [1]), (512, 512, list(range(2, 8))), (768, 256, list(range(8, 16)))],
    [(1024, 512, list(range(1, 16)))],
]


class Prog:
    ENGINES = ("sync", "tensor", "vector", "scalar", "gpsimd")

    def __init__(self, nc, stack):
        self.nc = nc
        self.stack = stack
        self.stream = {e: [] for e in self.ENGINES}
        self._sems = {}
        self.nbar = 0
        for s in ("tP", "tV", "tA", "tS", "dG"):
            h = stack.enter_context(nc.semaphore(s))
            self._sems[s] = [h, 0]

    def _sem(self, name):
        if name not in self._sems:
            h = self.stack.enter_context(self.nc.semaphore(name))
            self._sems[name] = [h, 0]
        return self._sems[name]

    def v(self, name):
        return self._sem(name)[1]

    def wait(self, eng, name, val=None):
        h, cur = self._sem(name)
        v = cur if val is None else val
        if v is None or v <= 0:
            return
        self.stream[eng].append(lambda e, h=h, v=v: e.wait_ge(h, v))

    def op(self, eng, fn):
        self.stream[eng].append(fn)

    def inc_last(self, eng, sem, n=1):
        rec = self._sem(sem)
        h = rec[0]
        fn = self.stream[eng].pop()
        self.stream[eng].append(lambda e, fn=fn, h=h, n=n: fn(e).then_inc(h, n))
        rec[1] += n
        return rec[1]

    def dma(self, eng, out, in_, ch="misc"):
        if eng == "gpsimd" and ch == "misc":
            ch = "dG"
        self.op(eng, lambda e, o=out, i=in_: e.dma_start(out=o, in_=i))
        return self.inc_last(eng, ch, 16)

    def barrier(self):
        for name, rec in list(self._sems.items()):
            if name in ("tP", "tV", "tA", "tS", "dG"):
                continue
            self.wait("sync", name, rec[1])
        self.op("sync", lambda e: e.sem_inc(self._sem("tS")[0], 1))
        self._sem("tS")[1] += 1
        self.wait("gpsimd", "dG", self.v("dG"))
        self.op("gpsimd", lambda e: e.sem_inc(self._sem("tS")[0], 1))
        self._sem("tS")[1] += 1
        ts = self.v("tS")
        tp, tv, ta = self.v("tP"), self.v("tV"), self.v("tA")
        for eng in self.ENGINES:
            self.wait(eng, "tS", ts)
            self.wait(eng, "tP", tp)
            self.wait(eng, "tV", tv)
            self.wait(eng, "tA", ta)


def build_nc():
    nc = bass.Bass("TRN2", target_bir_lowering=False, debug=False)

    d_x = nc.dram_tensor("x_full", [SEQ, D], F32, kind="ExternalInput").ap()
    d_xq = nc.dram_tensor("x_q", [NQ, D], F32, kind="ExternalInput").ap()
    d_wq = nc.dram_tensor("wq_f", [D, AQK], F32, kind="ExternalInput").ap()
    d_wk = nc.dram_tensor("wk_f", [D, AQK], F32, kind="ExternalInput").ap()
    d_wv = nc.dram_tensor("wv_f", [D, AQK], F32, kind="ExternalInput").ap()
    d_wo = nc.dram_tensor("wo_f", [AQK, D], F32, kind="ExternalInput").ap()
    d_fkv0 = nc.dram_tensor("fkv0", [1792, 64], F32, kind="ExternalInput").ap()
    d_fkv1 = nc.dram_tensor("fkv1", [1024, 128], F32, kind="ExternalInput").ap()
    d_pk0 = nc.dram_tensor("pk0bd", [32, 256], F32, kind="ExternalInput").ap()
    d_pv0 = nc.dram_tensor("pv0bd", [32, 256], F32, kind="ExternalInput").ap()
    d_pk1 = nc.dram_tensor("pk1bd", [64, 512], F32, kind="ExternalInput").ap()
    d_pv1 = nc.dram_tensor("pv1bd", [64, 512], F32, kind="ExternalInput").ap()
    d_lg = nc.dram_tensor("lg", [N_IT, 128, D], I32, kind="ExternalInput").ap()
    d_lu = nc.dram_tensor("lu", [N_IT, 128, D], I32, kind="ExternalInput").ap()
    d_ld = nc.dram_tensor("ld_t", [N_HT, 128, 8192], I32, kind="ExternalInput").ap()
    d_gs = nc.dram_tensor("gsc_t", [128, N_IT], F32, kind="ExternalInput").ap()
    d_us = nc.dram_tensor("usc_t", [128, N_IT], F32, kind="ExternalInput").ap()
    d_ds = nc.dram_tensor("dsc_t", [128, N_HT], F32, kind="ExternalInput").ap()
    d_idb = nc.dram_tensor("identb", [128, 128], BF16, kind="ExternalInput").ap()
    d_idf = nc.dram_tensor("identf", [128, 128], F32, kind="ExternalInput").ap()
    d_ones = nc.dram_tensor("ones_r", [1, 128], F32, kind="ExternalInput").ap()
    d_masks = nc.dram_tensor("masks", [128, 4096], F32, kind="ExternalInput").ap()
    d_pad = nc.dram_tensor("padrow", [NSLOT, 128, 64], F32, kind="ExternalInput").ap()

    d_out = nc.dram_tensor("out_q", [NQ, D], F32, kind="ExternalOutput").ap()

    KIND = "ExternalOutput" if DEBUG_TAPS else "Internal"
    s_k = nc.dram_tensor("k_scr", [AQK, SEQ], BF16, kind=KIND).ap()
    s_v = nc.dram_tensor("v_scr", [SEQ // 128, 128, 1920], BF16, kind=KIND).ap()
    s_kd = nc.dram_tensor("kd_scr", [AQK, NQ], BF16, kind=KIND).ap()
    s_vd = nc.dram_tensor("vd_scr", [NQ // 128, 128, 1920], BF16, kind=KIND).ap()
    s_xqf = nc.dram_tensor("xqf_scr", [N_HT, 128, NQ], BF16, kind=KIND).ap()
    s_at = nc.dram_tensor("at_scr", [AQK, NQ], BF16, kind=KIND).ap()
    s_o = nc.dram_tensor("o_scr", [N_HT, 128, NQ], F32, kind=KIND).ap()
    s_h = nc.dram_tensor("h_scr", [N_IT, 128, NQ], BF16, kind=KIND).ap()
    s_wkdump = (nc.dram_tensor("wk_dump", [2, 128, 16896], BF16,
                               kind="ExternalOutput").ap() if DEBUG_TAPS else None)

    stack = contextlib.ExitStack()
    p = Prog(nc, stack)
    last = {}

    g_idb = stack.enter_context(nc.sbuf_tensor("g_idb", [128, 128], BF16))
    g_idf = stack.enter_context(nc.sbuf_tensor("g_idf", [128, 128], F32))
    g_ones = stack.enter_context(nc.sbuf_tensor("g_ones", [1, 128], F32))
    g_eps = stack.enter_context(nc.sbuf_tensor("g_eps", [128, 1], F32))
    g_pad = stack.enter_context(nc.sbuf_tensor("g_pad", [128, NSLOT * 64], F32))

    arena = stack.enter_context(nc.psum_tensor("arena", [128, 4096], F32))

    def A(bank, cols=512, rows=128):
        return arena[0:rows, bank * 512: bank * 512 + cols]

    p.dma("sync", g_idb[:], d_idb, "cst")
    p.dma("sync", g_idf[:], d_idf, "cst")
    p.dma("sync", g_ones[:], d_ones, "cst")
    for j in range(NSLOT):
        p.dma("sync", g_pad[:, j * 64:j * 64 + 64], d_pad[j, :, :], "cst")
    consts_cst = p.v("cst")
    p.op("vector", lambda e: e.memset(g_eps[:], 1e-5))
    p.inc_last("vector", "tV")

    qkv_off = {}
    off = 0
    for ht in range(N_HT):
        a0, ncol = qkv_ht_cols(ht)
        qkv_off[ht] = (off, a0)
        off += ncol
    QKV_PACK = off


    qt_cm = nc.sbuf_tensor("qt_sb", [128, N_AT * NQ], BF16)
    qt_sb = qt_cm.__enter__()
    stageP = contextlib.ExitStack()
    xbuf = stageP.enter_context(nc.sbuf_tensor("xbuf", [128, 2 * D], F32))
    sq = stageP.enter_context(nc.sbuf_tensor("sq", [128, D], BF16))
    xnt = stageP.enter_context(nc.sbuf_tensor("xnt", [128, 4 * D], BF16))
    lnS = stageP.enter_context(nc.sbuf_tensor("lnS", [128, 6 * 140], F32))
    xnfm = stageP.enter_context(nc.sbuf_tensor("xnfm", [128, N_HT * 512], BF16))
    c0sb = stageP.enter_context(nc.sbuf_tensor("c0sb", [64, 512], BF16))
    c1sb = stageP.enter_context(nc.sbuf_tensor("c1sb", [128, 512], BF16))
    kstag = stageP.enter_context(nc.sbuf_tensor("kstag", [128, 2 * 512], BF16))
    vstag = stageP.enter_context(nc.sbuf_tensor("vstag", [128, 2 * 1920], BF16))


    for par in range(2):
        vv = vstag.ap().rearrange("p (r h d) -> p r h d", r=2, h=24)
        p.op("vector", lambda e, vv=vv, par=par: e.memset(vv[:, par, :, 64:80], 0.0))
        tvm = p.inc_last("vector", "tV")
        p.wait("vector", "tV", tvm)
        p.op("vector", lambda e, vv=vv, par=par: e.memset(vv[:, par, :, 64:65], 1.0))
        p.inc_last("vector", "tV")
    vstag_init_tV = p.v("tV")

    stagePA = contextlib.ExitStack()
    w_k = stagePA.enter_context(nc.sbuf_tensor("w_k", [128, QKV_PACK], BF16))
    w_v = stagePA.enter_context(nc.sbuf_tensor("w_v", [128, QKV_PACK], BF16))
    w_q = None
    w_f0 = stagePA.enter_context(nc.sbuf_tensor("w_f0", [128, 14 * 64], BF16))
    w_f1 = stagePA.enter_context(nc.sbuf_tensor("w_f1", [128, 8 * 128], BF16))
    w_pk0 = stagePA.enter_context(nc.sbuf_tensor("w_pk0", [32, 256], BF16))
    w_pv0 = stagePA.enter_context(nc.sbuf_tensor("w_pv0", [64, 256], BF16))
    w_pk1 = stagePA.enter_context(nc.sbuf_tensor("w_pk1", [64, 512], BF16))
    w_pv1 = stagePA.enter_context(nc.sbuf_tensor("w_pv1", [128, 512], BF16))

    def wslice(sb, ht, c0, ncols):
        o, a0 = qkv_off[ht]
        return sb[:, o + c0 - a0: o + c0 - a0 + ncols]

    for (sb, dr) in ((w_k, d_wk), (w_v, d_wv)):
        for ht in range(N_HT):
            o, a0 = qkv_off[ht]
            ncol = qkv_ht_cols(ht)[1]
            p.dma("gpsimd", sb[:, o:o + ncol], dr[ht * 128:(ht + 1) * 128, a0:a0 + ncol])
    for ht in range(14):
        p.dma("gpsimd", w_f0[:, ht * 64:(ht + 1) * 64], d_fkv0[ht * 128:(ht + 1) * 128, :])
    for ht in range(8):
        p.dma("gpsimd", w_f1[:, ht * 128:(ht + 1) * 128], d_fkv1[ht * 128:(ht + 1) * 128, :])
    p.dma("gpsimd", w_pk0[:], d_pk0)
    p.dma("gpsimd", w_pv0[32:64, :], d_pv0)
    p.dma("gpsimd", w_pk1[:], d_pk1)
    p.dma("gpsimd", w_pv1[64:128, :], d_pv1)
    weights_dG = p.v("dG")

    pend_st = []

    def flush_st():
        for f in pend_st:
            f()
        del pend_st[:]

    ln_n = [0]

    def ln_tile(sl, sl4):
        b = 6 * ln_n[0]
        ln_n[0] += 1
        sv, ss, rs, nm, nmrs, lnv = (lnS[:, b + i:b + i + 1] for i in range(6))
        xap = xbuf[:, sl * D:(sl + 1) * D]
        xnap = xnt[:, sl4 * D:(sl4 + 1) * D]
        p.op("scalar", lambda e: e.activation(sq[:], xap, AF.Copy, scale=-1.0,
                                              accum_out=sv))
        p.op("scalar", lambda e: e.drain())
        p.op("scalar", lambda e: e.activation(sq[:], xap, AF.Square, bias=sv,
                                              scale=float(D), accum_out=ss))
        p.op("scalar", lambda e: e.drain())
        p.op("scalar", lambda e: e.activation(lnv, ss, AF.Ln, bias=g_eps[:],
                                              scale=1.0 / float(D) ** 3))
        p.op("scalar", lambda e: e.drain())
        p.op("scalar", lambda e: e.activation(rs, lnv, AF.Exp, scale=-0.5))
        p.op("scalar", lambda e: e.activation(nm, sv, AF.Copy, scale=1.0 / D))
        p.op("scalar", lambda e: e.drain())
        p.op("scalar", lambda e: e.activation(nmrs, nm, AF.Copy, scale=rs))
        p.op("scalar", lambda e: e.drain())
        p.wait("scalar", "tP", last.get(("xnt_rd", sl4), 0))
        p.op("scalar", lambda e: e.activation(xnap, xap, AF.Identity,
                                              scale=rs, bias=nmrs))
        return p.inc_last("scalar", "tA")

    def proj_chunk(src, ci, mode):
        kdst = {"kv": s_k, "kvd": s_kd}.get(mode)
        vdst = {"kv": s_v, "kvd": s_vd}.get(mode)
        tok0 = ci * 512

        for tt in range(4):
            g = ci * 4 + tt
            sl = g % 2
            xld = p.dma("scalar", xbuf[:, sl * D:(sl + 1) * D],
                        src[tok0 + tt * 128: tok0 + (tt + 1) * 128, :], "x%d" % sl)
            p.wait("scalar", "x%d" % sl, xld)
            ta_ln = ln_tile(sl, tt)
            sb = (tt % 2) * 4
            p.wait("tensor", "tA", ta_ln)
            p.wait("tensor", "tV", last.get(("tevac", sb), 0))
            if sb == 0:
                p.wait("tensor", "tV", last.get(("slots03_rd", ), 0))
            else:
                p.wait("tensor", "tV", last.get(("slots47_rd", ), 0))
            for ht in range(N_HT):
                dst = arena[:, sb * 512 + ht * 128: sb * 512 + (ht + 1) * 128]
                lhs = xnt[:, tt * D + ht * 128: tt * D + (ht + 1) * 128]
                p.op("tensor", lambda e, d=dst, l=lhs: e.matmul(d, l, g_idb[:],
                                                                start=True, stop=True))
            tp = p.inc_last("tensor", "tP")
            last[("xnt_rd", tt)] = tp
            p.wait("vector", "tP", tp)
            p.wait("vector", "tP", last.get(("xnfm_free_tP", ), 0))
            p.wait("vector", "xqf", last.get(("xnfm_free_dS", ), 0))
            for s4 in range(4):
                src_ap = arena[:, (sb + s4) * 512:(sb + s4 + 1) * 512].rearrange(
                    "p (h t) -> p h t", h=4)
                dst_ap = xnfm.ap().rearrange("p (h t) -> p h t", h=N_HT)[
                    :, s4 * 4:(s4 + 1) * 4, tt * 128:(tt + 1) * 128]
                p.op("vector", lambda e, s=src_ap, d=dst_ap: e.tensor_copy(d, s))
            ta = p.inc_last("vector", "tV")
            last[("tevac", sb)] = ta
        xnfm_tA = p.v("tV")

        if mode != "q":
            p.wait("tensor", "tV", xnfm_tA)
            p.wait("tensor", "tV", last.get(("cevac", ), 0))
            for i, ht in enumerate(range(2, 16)):
                lhs = w_f0[:, (ht - 2) * 64:(ht - 1) * 64]
                rhs = xnfm[:, ht * 512:(ht + 1) * 512]
                p.op("tensor", lambda e, l=lhs, r=rhs, s=(i == 0): e.matmul(
                    A(0, 512, 64), l, r, start=s, stop=(ht == 15)))
            for i, ht in enumerate(range(8, 16)):
                lhs = w_f1[:, (ht - 8) * 128:(ht - 7) * 128]
                rhs = xnfm[:, ht * 512:(ht + 1) * 512]
                p.op("tensor", lambda e, l=lhs, r=rhs, s=(i == 0): e.matmul(
                    A(1, 512, 128), l, r, start=s, stop=(ht == 15)))
            tp = p.inc_last("tensor", "tP")
            p.wait("vector", "tP", tp)
            p.op("vector", lambda e: e.tensor_copy(c0sb[:], A(0, 512, 64)))
            p.op("vector", lambda e: e.tensor_copy(c1sb[:], A(1, 512, 128)))
            cev = p.inc_last("vector", "tV")
            last[("cevac", )] = cev
        else:
            cev = 0
            p.wait("tensor", "tV", xnfm_tA)

        wsb = w_q if mode == "q" else w_k
        p.wait("tensor", "tV", cev)
        for at in range(N_AT):
            slot = 2 + (at % 2)
            p.wait("tensor", "tV", last.get(("kevac", at % 2), 0))
            hts = list(qkv_at_hts(at))
            has_corr = (mode != "q") and (at < 6)
            for i, ht in enumerate(hts):
                lhs = wslice(wsb, ht, at * 128, 128)
                rhs = xnfm[:, ht * 512:(ht + 1) * 512]
                p.op("tensor", lambda e, d=A(slot), l=lhs, r=rhs,
                     s=(i == 0), st=(i == len(hts) - 1 and not has_corr): e.matmul(
                    d, l, r, start=s, stop=st))
            if has_corr:
                if at < 2:
                    lhs = w_pk0[:, at * 128:(at + 1) * 128]
                    p.op("tensor", lambda e, d=A(slot), l=lhs: e.matmul(
                        d, l, c0sb[0:32, :], start=False, stop=True))
                else:
                    lhs = w_pk1[:, (at - 2) * 128:(at - 1) * 128]
                    p.op("tensor", lambda e, d=A(slot), l=lhs: e.matmul(
                        d, l, c1sb[0:64, :], start=False, stop=True))
            tp = p.inc_last("tensor", "tP")
            p.wait("vector", "tP", tp)
            if mode == "q":
                dst = qt_sb[:, at * NQ + ci * 512: at * NQ + (ci + 1) * 512]
                p.op("vector", lambda e, d=dst, s=A(slot): e.tensor_copy(d, s))
                last[("kevac", at % 2)] = p.inc_last("vector", "tV")
            else:
                ks = kstag[:, (at % 2) * 512:(at % 2 + 1) * 512]
                p.wait("vector", "k%d" % (at % 2), last.get(("kdma", at % 2), 0))
                p.op("vector", lambda e, d=ks, s=A(slot): e.tensor_copy(d, s))
                tv = p.inc_last("vector", "tV")
                last[("kevac", at % 2)] = tv
                p.wait("sync", "tV", tv)
                dd = p.dma("sync", kdst[at * 128:(at + 1) * 128, ci * 512:(ci + 1) * 512],
                           ks, "k%d" % (at % 2))
                last[("kdma", at % 2)] = dd
        last[("slots03_rd", )] = p.v("tV")

        if mode == "q":
            p.wait("sync", "tV", xnfm_tA)
            for ht in range(N_HT):
                dd = p.dma("sync", s_xqf[ht, :, ci * 512:(ci + 1) * 512],
                           xnfm[:, ht * 512:(ht + 1) * 512], "xqf")
            last[("xnfm_free_dS", )] = dd
            last[("xnfm_free_tP", )] = p.v("tP")
            last[("slots47_rd", )] = p.v("tV")
            return

        vdst_scr = s_v if mode == "kv" else s_vd
        for tt in range(4):
            p.wait("tensor", "tV", last.get(("vevac", ), 0))
            for bank in range(3):
                lhsT = xnfm[:, 0 * 512 + tt * 128: 0 * 512 + tt * 128 + 128]
                rhs = wslice(w_v, 0, bank * 512, 512)
                p.op("tensor", lambda e, d=A(4 + bank), l=lhsT, r=rhs: e.matmul(
                    d, l, r, start=True, stop=False))
                for (c0, clen, hts) in V_BANK_PLAN[bank]:
                    for ht in hts:
                        lhsT = xnfm[:, ht * 512 + tt * 128: ht * 512 + tt * 128 + 128]
                        rhs = wslice(w_v, ht, c0, clen)
                        dst = arena[:, 4 * 512 + c0: 4 * 512 + c0 + clen]
                        p.op("tensor", lambda e, d=dst, l=lhsT, r=rhs: e.matmul(
                            d, l, r, start=False, stop=False))
            p.op("tensor", lambda e, l=c0sb[32:64, tt * 128:(tt + 1) * 128]: e.matmul(
                arena[:, 4 * 512: 4 * 512 + 256], l, w_pv0[32:64, :], start=False, stop=False))
            p.op("tensor", lambda e, l=c1sb[64:128, tt * 128:(tt + 1) * 128]: e.matmul(
                arena[:, 4 * 512 + 256: 4 * 512 + 768], l, w_pv1[64:128, :], start=False, stop=True))
            tp = p.inc_last("tensor", "tP")
            vsl = tt % 2
            p.wait("vector", "tP", tp)
            p.wait("vector", "v%d" % vsl, last.get(("vdma", vsl), 0))
            vsrc = arena[:, 4 * 512: 4 * 512 + 1536].rearrange("p (h d) -> p h d", h=24)
            vdst = vstag.ap().rearrange("p (r h d) -> p r h d", r=2, h=24)[:, vsl, :, 0:64]
            p.op("vector", lambda e, d=vdst, s=vsrc: e.tensor_copy(d, s))
            tv = p.inc_last("vector", "tV")
            last[("vevac", )] = tv
            p.wait("sync", "tV", tv)
            gt = ci * 4 + tt
            dd = p.dma("sync", vdst_scr[gt, :, :],
                       vstag[:, vsl * 1920:(vsl + 1) * 1920], "v%d" % vsl)
            last[("vdma", vsl)] = dd

        last[("xnfm_free_tP", )] = p.v("tP")
        last[("xnfm_free_dS", )] = 0
        last[("slots47_rd", )] = p.v("tV")

    p.wait("tensor", "dG", weights_dG)
    p.wait("tensor", "cst", consts_cst)
    p.wait("tensor", "tV", vstag_init_tV)
    p.wait("scalar", "cst", consts_cst)
    if DEBUG_TAPS:
        p.wait("sync", "dG", weights_dG)
        p.dma("sync", s_wkdump[0, :, :], w_k[:], "wkd")
        p.dma("sync", s_wkdump[1, :, :], w_v[:], "wkd")

    for ci in range(16):
        proj_chunk(d_x, ci, "kv")
    for ci in range(4):
        proj_chunk(d_xq, ci, "kvd")

    stagePA.close()
    stagePB = contextlib.ExitStack()
    w_q = stagePB.enter_context(nc.sbuf_tensor("w_q", [128, QKV_PACK], BF16))
    p.wait("gpsimd", "tP", p.v("tP"))
    for ht in range(N_HT):
        o, a0 = qkv_off[ht]
        ncol = qkv_ht_cols(ht)[1]
        p.dma("gpsimd", w_q[:, o:o + ncol], d_wq[ht * 128:(ht + 1) * 128, a0:a0 + ncol])
    p.wait("tensor", "dG", p.v("dG"))
    for ci in range(4):
        proj_chunk(d_xq, ci, "q")
    stagePB.close()
    stageP.close()


    p.barrier()

    stage3 = contextlib.ExitStack()
    masks_sb = stage3.enter_context(nc.sbuf_tensor("masks_sb", [128, 4096], F32))
    kbuf = stage3.enter_context(nc.sbuf_tensor("kbuf", [128, 2 * 256], BF16))
    vbuf = stage3.enter_context(nc.sbuf_tensor("vbuf", [128, 2 * 320], BF16))
    ubuf = stage3.enter_context(nc.sbuf_tensor("ubuf", [128, 4 * 1024], BF16))
    bb = stage3.enter_context(nc.sbuf_tensor("bb", [128, 64 * NSLOT * N_PAIR], F32))
    cb = stage3.enter_context(nc.sbuf_tensor("cb", [128, N_PAIR], F32))
    smax = stage3.enter_context(nc.sbuf_tensor("smax", [128, N_PAIR], F32))
    gmx = stage3.enter_context(nc.sbuf_tensor("gmx", [1, 2 * N_PAIR], F32))
    rc = stage3.enter_context(nc.sbuf_tensor("rc", [1, 512], F32))
    bcsb = stage3.enter_context(nc.sbuf_tensor("bcsb", [64, 512], F32))
    at_sb = stage3.enter_context(nc.sbuf_tensor("at_sb", [64, 2 * 512], BF16))
    xq_all = stage3.enter_context(nc.sbuf_tensor("xq_all", [128, N_HT * 2048], BF16))
    wgb = stage3.enter_context(nc.sbuf_tensor("wgb", [128, 2 * 2048], BF16))
    wub = stage3.enter_context(nc.sbuf_tensor("wub", [128, 2 * 2048], BF16))
    t1f = stage3.enter_context(nc.sbuf_tensor("t1f", [128, 2 * 512], F32))
    hbb = stage3.enter_context(nc.sbuf_tensor("hbb", [128, 2 * 512], BF16))
    gsb = stage3.enter_context(nc.sbuf_tensor("gsb", [128, N_IT], F32))
    usb = stage3.enter_context(nc.sbuf_tensor("usb", [128, N_IT], F32))

    p.dma("sync", masks_sb[:], d_masks, "msk")
    masks_ch = p.v("msk")
    p.wait("scalar", "msk", masks_ch)
    p.wait("vector", "msk", masks_ch)
    p.dma("sync", gsb[:], d_gs, "xqa")
    p.dma("sync", usb[:], d_us, "xqa")
    for ht in range(N_HT):
        p.dma("sync", xq_all[:, ht * 2048:(ht + 1) * 2048], s_xqf[ht, :, :], "xqa")
    xqa_ch = p.v("xqa")
    p.wait("tensor", "xqa", xqa_ch)
    p.wait("scalar", "xqa", xqa_ch)
    p.wait("vector", "xqa", xqa_ch)

    hcnt = [0]

    def emit_gu_load(it):
        if it >= N_IT:
            return
        sl = it % 2
        p.wait("gpsimd", "tV", last.get(("guconv", sl), 0))
        p.wait("gpsimd", "tP", last.get(("gumm", sl), 0))
        p.dma("gpsimd", wgb[:, sl * 2048:(sl + 1) * 2048], d_lg[it, :, :], "lw%d" % sl)
        uld = p.dma("gpsimd", wub[:, sl * 2048:(sl + 1) * 2048], d_lu[it, :, :], "lw%d" % sl)
        last[("guld", sl)] = uld

    def emit_gu_conv(it):
        sl = it % 2
        p.wait("vector", "lw%d" % sl, last[("guld", sl)])
        p.wait("vector", "tP", last.get(("gumm", sl), 0))
        p.op("vector", lambda e, s=sl: e.tensor_scalar_add(
            wgb[:, s * 2048:(s + 1) * 2048], wgb[:, s * 2048:(s + 1) * 2048], -128.0))
        p.op("vector", lambda e, s=sl: e.tensor_scalar_add(
            wub[:, s * 2048:(s + 1) * 2048], wub[:, s * 2048:(s + 1) * 2048], -128.0))
        cvt = p.inc_last("vector", "tV")
        last[("guconv", sl)] = cvt

    def gu_quanta():
        emit_gu_load(0)
        emit_gu_conv(0)
        emit_gu_load(1)
        for it in range(N_IT):
            sl = it % 2
            for tc in range(4):
                tsl = tc % 2
                def q(it=it, sl=sl, tc=tc, tsl=tsl):
                    flush_st()
                    p.wait("tensor", "tV", last.get(("guconv", sl), 0))
                    p.wait("tensor", "tA", last.get(("b7silu",), 0))
                    for ht in range(N_HT):
                        lhs = wgb[:, sl * 2048 + ht * 128: sl * 2048 + (ht + 1) * 128]
                        rhs = xq_all[:, ht * 2048 + tc * 512: ht * 2048 + (tc + 1) * 512]
                        p.op("tensor", lambda e, d=arena[:, 7 * 512:8 * 512],
                             l=lhs, r=rhs, st=(ht == 0), sp=(ht == 15): e.matmul(
                            d, l, r, start=st, stop=sp))
                    tg = p.inc_last("tensor", "tP")
                    p.wait("tensor", "tV", last.get(("b6sts",), 0))
                    p.wait("tensor", "tA", last.get(("b6a",), 0))
                    p.wait("tensor", "tV", last.get(("b6v",), 0))
                    for ht in range(N_HT):
                        lhs = wub[:, sl * 2048 + ht * 128: sl * 2048 + (ht + 1) * 128]
                        rhs = xq_all[:, ht * 2048 + tc * 512: ht * 2048 + (tc + 1) * 512]
                        p.op("tensor", lambda e, d=arena[:, 6 * 512:7 * 512],
                             l=lhs, r=rhs, st=(ht == 0), sp=(ht == 15): e.matmul(
                            d, l, r, start=st, stop=sp))
                    tu = p.inc_last("tensor", "tP")
                    last[("gumm", sl)] = tu
                    p.wait("scalar", "tP", tg)
                    p.wait("scalar", "tV", last.get(("t1rd", tsl), 0))
                    p.op("scalar", lambda e: e.activation(
                        t1f[:, tsl * 512:(tsl + 1) * 512],
                        arena[:, 7 * 512:8 * 512], AF.Silu,
                        scale=gsb[:, it:it + 1]))
                    last[("b7silu",)] = p.inc_last("scalar", "tA")
                    p.wait("vector", "tP", tu)
                    p.wait("vector", "tA", last[("b7silu",)])
                    p.wait("vector", "h%d" % tsl, last.get(("hdma", tsl), 0))
                    p.op("vector", lambda e: e.scalar_tensor_tensor(
                        hbb[:, tsl * 512:(tsl + 1) * 512],
                        arena[:, 6 * 512:7 * 512], usb[:, it:it + 1],
                        t1f[:, tsl * 512:(tsl + 1) * 512], ALU.mult, ALU.mult))
                    tvh = p.inc_last("vector", "tV")
                    last[("b6sts",)] = tvh
                    last[("t1rd", tsl)] = tvh

                    def st(it=it, tc=tc, tsl=tsl, tvh=tvh):
                        p.wait("sync", "tV", tvh)
                        dd = p.dma("sync", s_h[it, :, tc * 512:(tc + 1) * 512],
                                   hbb[:, tsl * 512:(tsl + 1) * 512], "h%d" % tsl)
                        last[("hdma", tsl)] = dd
                    pend_st.append(st)
                    if tc == 3:
                        if it + 1 < N_IT:
                            emit_gu_conv(it + 1)
                        emit_gu_load(it + 2)
                yield q
    gu_gen = gu_quanta()
    gu_left = [bool(GU_ENABLE)]
    gu_slot = [0]

    def emit_gu_quantum():
        if not gu_left[0]:
            return
        try:
            next(gu_gen)()
        except StopIteration:
            gu_left[0] = False

    at_dma_n = 0

    for hp in range(N_PAIR):
        cbA = cb[:, hp:hp + 1]
        for j in range(NSLOT):
            bbA = bb[:, (hp * NSLOT + j) * 64:(hp * NSLOT + j) * 64 + 64]
            nch = 16 * j + 12
            tp_av = last.get(("avP",), 0)
            pend_av = []
            for ch in range(nch + 4):
                diag = ch >= nch
                chd = ch - nch
                par = ch % 2
                usl = ch % 4
                Sap = arena[:, par * 1024:(par + 1) * 1024]
                if ch % 2 == 0:
                    kl = ch // 2
                    ksl = kl % 2
                    p.wait("sync", "tP", last.get(("smm", ksl), 0))
                    if not diag:
                        ksrc = s_k[hp * 128:(hp + 1) * 128, kl * 256:(kl + 1) * 256]
                    else:
                        ksrc = s_kd[hp * 128:(hp + 1) * 128,
                                    j * 512 + (chd // 2) * 256:j * 512 + (chd // 2 + 1) * 256]
                    kld = p.dma("sync", kbuf[:, ksl * 256:(ksl + 1) * 256], ksrc,
                                "kb%d" % ksl)
                    last[("kld", ksl)] = kld
                    p.wait("sync", "tP", last.get(("avv", ksl), 0))
                    vsc = s_v if not diag else s_vd
                    tt0 = (2 * kl) if not diag else (4 * j + 2 * (chd // 2))
                    vsrc = vsc[tt0:tt0 + 2, :, hp * 160:(hp + 1) * 160].rearrange(
                        "t p d -> p t d")
                    vd3 = vbuf.ap().rearrange("p (r t d) -> p r t d", r=2, t=2)[:, ksl]
                    vld = p.dma("sync", vd3, vsrc, "vb%d" % ksl)
                    last[("vld", ksl)] = vld
                kl = ch // 2
                ksl = kl % 2
                kap = kbuf[:, ksl * 256 + (ch % 2) * 128: ksl * 256 + (ch % 2) * 128 + 128]
                p.wait("tensor", "tA", last.get(("exp", par), 0))
                p.wait("tensor", "kb%d" % ksl, last[("kld", ksl)])
                for h2 in (0, 1):
                    lhs = kap[h2 * 64:(h2 + 1) * 64, :] if False else None
                    p.op("tensor", lambda e,
                         l=kbuf[h2 * 64:(h2 + 1) * 64,
                                ksl * 256 + (ch % 2) * 128: ksl * 256 + (ch % 2) * 128 + 128],
                         r=qt_sb[h2 * 64:(h2 + 1) * 64,
                                 hp * NQ + j * 512: hp * NQ + (j + 1) * 512],
                         d=arena[:, par * 1024 + h2 * 512: par * 1024 + (h2 + 1) * 512],
                         tp=((64, 0) if h2 else None): e.matmul(
                        d, l, r, start=True, stop=True, tile_position=tp))
                tp_s = p.inc_last("tensor", "tP")
                last[("smm", ksl)] = tp_s

                if j == 0 and ch == 0:
                    smx = smax[:, hp:hp + 1]
                    gmx0 = gmx[0:1, 2 * hp:2 * hp + 1]
                    gmx1 = gmx[0:1, 2 * hp + 1:2 * hp + 2]
                    p.wait("vector", "tP", tp_s)
                    p.op("vector", lambda e, o=smx: e.reduce_max(
                        o, arena[:, 0:1024:4], axis=AX.X))
                    p.op("vector", lambda e: e.drain())
                    tv = p.inc_last("vector", "tV")
                    p.wait("tensor", "tV", tv)
                    p.wait("tensor", "tA", last.get(("b6a",), 0))
                    p.wait("tensor", "tV", last.get(("b6v",), 0))
                    p.wait("tensor", "tV", last.get(("b6sts",), 0))
                    p.op("tensor", lambda e, l=smx: e.matmul(
                        arena[0:1, 6 * 512:6 * 512 + 128], l, g_idf[:],
                        start=True, stop=True))
                    tp2 = p.inc_last("tensor", "tP")
                    p.wait("vector", "tP", tp2)
                    p.op("vector", lambda e, o=gmx0: e.reduce_max(
                        o, arena[0:1, 6 * 512:6 * 512 + 128], axis=AX.X))
                    p.op("vector", lambda e: e.drain())
                    p.op("vector", lambda e, o=gmx1, i=gmx0: e.tensor_scalar(
                        o, i, -0.125, -MARGIN, ALU.mult, ALU.add))
                    p.op("vector", lambda e: e.drain())
                    tv2 = p.inc_last("vector", "tV")
                    last[("b6v",)] = tv2
                    p.wait("tensor", "tV", tv2)
                    p.op("tensor", lambda e, r=gmx1: e.matmul(
                        arena[0:128, 6 * 512:6 * 512 + 1], g_ones[:, :],
                        r, start=True, stop=True))
                    tp3 = p.inc_last("tensor", "tP")
                    p.wait("scalar", "tP", tp3)
                    p.op("scalar", lambda e, o=cbA: e.copy(o, arena[0:128, 6 * 512:6 * 512 + 1]))
                    cbA_tA = p.inc_last("scalar", "tA")
                    last[("cbA",)] = cbA_tA
                    last[("b6a",)] = cbA_tA
                    p.wait("vector", "tA", cbA_tA)
                    for jj in range(NSLOT):
                        p.op("vector", lambda e, o=bb[:, (hp * NSLOT + jj) * 64:
                                                      (hp * NSLOT + jj) * 64 + 64],
                             c=cbA, jx=jj: e.tensor_scalar(
                            o, g_pad[:, jx * 64:(jx + 1) * 64], c, None, ALU.add))
                    p.op("vector", lambda e: e.drain())
                    last[("bbV", hp)] = p.inc_last("vector", "tV")

                if diag:
                    p.wait("vector", "tP", tp_s)
                    p.op("vector", lambda e, cd=chd, S=Sap: e.tensor_tensor(
                        S, S, masks_sb[:, cd * 1024:(cd + 1) * 1024], ALU.add))
                    last[("mskV",)] = p.inc_last("vector", "tV")

                p.wait("scalar", "tP", tp_s)
                if diag:
                    p.wait("scalar", "tV", last[("mskV",)])
                p.wait("scalar", "tV", last.get(("bbV", hp), 0))
                p.wait("scalar", "tP", last.get(("avu", usl), 0))
                bias_ap = cbA if diag else bbA[:, ch:ch + 1]
                p.op("scalar", lambda e, b=bias_ap, u=usl, S=Sap: e.activation(
                    ubuf[:, u * 1024:(u + 1) * 1024], S, AF.Exp,
                    bias=b, scale=0.125))
                last[("exp", par)] = p.inc_last("scalar", "tA")

                def emit_av(ch=ch, ksl=ksl, usl=usl, exp_ta=last[("exp", par)],
                            vld=last[("vld", ksl)], is_last=(ch == nch + 3),
                            avrd=(last.get(("avrd",), 0) if ch == 0 else 0)):
                    p.wait("tensor", "tA", exp_ta)
                    p.wait("tensor", "vb%d" % ksl, vld)
                    if ch == 0:
                        p.wait("tensor", "tV", avrd)
                    for h2 in (0, 1):
                        lhs = vbuf[:, ksl * 320 + (ch % 2) * 160 + h2 * 80:
                                   ksl * 320 + (ch % 2) * 160 + h2 * 80 + 65]
                        rhs = ubuf[:, usl * 1024 + h2 * 512: usl * 1024 + (h2 + 1) * 512]
                        dst = arena[0:65, (4 + h2) * 512:(4 + h2) * 512 + 512]
                        p.op("tensor", lambda e, d=dst, l=lhs, r=rhs,
                             st=(ch == 0), sp=(is_last and h2 == 1): e.matmul(
                            d, l, r, start=st, stop=sp))
                    t = p.inc_last("tensor", "tP")
                    last[("avu", usl)] = t
                    last[("avv", ksl)] = t
                    return t
                pend_av.append(emit_av)
                while len(pend_av) > 2:
                    tp_av = pend_av.pop(0)()
                if ch % 2 == 1:
                    gu_slot[0] += 1
                    if gu_slot[0] % 6 == 1:
                        emit_gu_quantum()
            while pend_av:
                tp_av = pend_av.pop(0)()
            last[("avP",)] = tp_av
            for h2 in (0, 1):
                flush_st()
                avp = arena[0:65, (4 + h2) * 512:(4 + h2) * 512 + 512]
                p.wait("vector", "tP", tp_av)
                p.op("vector", lambda e, a=avp: e.reciprocal(rc[:], a[64:65, :]))
                tvr = p.inc_last("vector", "tV")
                emit_gu_quantum()
                p.wait("tensor", "tV", tvr)
                p.wait("tensor", "tA", last.get(("b6a",), 0))
                p.wait("tensor", "tV", last.get(("b6sts",), 0))
                p.op("tensor", lambda e: e.matmul(
                    arena[0:64, 6 * 512:6 * 512 + 512], g_ones[0:1, 0:64],
                    rc[0:1, :], start=True, stop=True))
                tpb = p.inc_last("tensor", "tP")
                p.wait("scalar", "tP", tpb)
                p.op("scalar", lambda e: e.copy(bcsb[:], arena[0:64, 6 * 512:6 * 512 + 512]))
                tab = p.inc_last("scalar", "tA")
                last[("b6a",)] = tab
                asl = at_dma_n % 2
                p.wait("vector", "tA", tab)
                p.wait("vector", "at%d" % asl, last.get(("atdma", asl), 0))
                p.op("vector", lambda e, a=avp, sl=asl: e.tensor_tensor(
                    at_sb[:, sl * 512:(sl + 1) * 512], a[0:64, :], bcsb[:], ALU.mult))
                tvn = p.inc_last("vector", "tV")
                head = 2 * hp + h2

                def atst(head=head, j=j, asl=asl, tvn=tvn):
                    p.wait("sync", "tV", tvn)
                    dd = p.dma("sync", s_at[head * 64:(head + 1) * 64,
                                            j * 512:(j + 1) * 512],
                               at_sb[:, asl * 512:(asl + 1) * 512], "at%d" % asl)
                    last[("atdma", asl)] = dd
                pend_st.append(atst)
                at_dma_n += 1
            last[("avrd",)] = p.v("tV")

    while gu_left[0]:
        emit_gu_quantum()
    flush_st()

    stage3.close()
    qt_cm.__exit__(None, None, None)
    p.barrier()

    stageC1 = contextlib.ExitStack()

    wo_off = {}
    off = 0
    for at in range(N_AT):
        if at < 2:
            h0, nc_ = 0, 2048
        elif at < 6:
            h0, nc_ = 256, 1792
        else:
            h0, nc_ = 1024, 1024
        wo_off[at] = (off, h0)
        off += nc_
    WO_PACK = off
    wo_sb = stageC1.enter_context(nc.sbuf_tensor("wo_sb", [128, WO_PACK], BF16))
    atc = stageC1.enter_context(nc.sbuf_tensor("atc", [128, N_AT * 512], BF16))
    ostg = stageC1.enter_context(nc.sbuf_tensor("ostg", [128, 2 * 512], F32))

    for at in range(N_AT):
        o, h0 = wo_off[at]
        nc_ = {0: 2048, 1: 2048}.get(at, 1792 if at < 6 else 1024)
        p.dma("gpsimd", wo_sb[:, o:o + nc_], d_wo[at * 128:(at + 1) * 128, h0:h0 + nc_])
    wo_dG = p.v("dG")
    p.wait("tensor", "dG", wo_dG)

    for tc in range(4):
        p.wait("sync", "tP", last.get(("oproj_mm",), 0))
        for at in range(N_AT):
            atl = p.dma("sync", atc[:, at * 512:(at + 1) * 512],
                        s_at[at * 128:(at + 1) * 128, tc * 512:(tc + 1) * 512], "atc")
        p.wait("tensor", "atc", atl)
        for ht in range(N_HT):
            flush_st()
            par = ht % 2
            ats = list(wo_ht_ats(ht))
            p.wait("tensor", "tV", last.get(("oev", par), 0))
            for i, at in enumerate(ats):
                o, h0 = wo_off[at]
                lhs = wo_sb[:, o + ht * 128 - h0: o + ht * 128 - h0 + 128]
                p.op("tensor", lambda e, d=arena[:, par * 512:(par + 1) * 512],
                     l=lhs, r=atc[:, at * 512:(at + 1) * 512],
                     st=(i == 0), sp=(i == len(ats) - 1): e.matmul(
                    d, l, r, start=st, stop=sp))
            tpo = p.inc_last("tensor", "tP")
            last[("oproj_mm",)] = tpo
            p.wait("vector", "tP", tpo)
            p.wait("vector", "o%d" % par, last.get(("odma", par), 0))
            p.op("vector", lambda e, pr=par: e.tensor_copy(
                ostg[:, pr * 512:(pr + 1) * 512], arena[:, pr * 512:(pr + 1) * 512]))
            tvo = p.inc_last("vector", "tV")
            last[("oev", par)] = tvo

            def ost(ht=ht, tc=tc, par=par, tvo=tvo):
                p.wait("sync", "tV", tvo)
                dd = p.dma("sync", s_o[ht, :, tc * 512:(tc + 1) * 512],
                           ostg[:, par * 512:(par + 1) * 512], "o%d" % par)
                last[("odma", par)] = dd
            pend_st.append(ost)

    flush_st()
    stageC1.close()
    p.barrier()

    stageC2 = contextlib.ExitStack()
    hh = stageC2.enter_context(nc.sbuf_tensor("hh", [128, N_IT * 512], BF16))
    wd_i = stageC2.enter_context(nc.sbuf_tensor("wd_i", [128, 6 * 2048], I32))
    wdb = stageC2.enter_context(nc.sbuf_tensor("wdb", [128, 6 * 2048], BF16))
    x2sb = stageC2.enter_context(nc.sbuf_tensor("x2sb", [128, 4 * 2048], F32))
    osb = stageC2.enter_context(nc.sbuf_tensor("osb", [128, 2 * 512], F32))
    s_sb = stageC2.enter_context(nc.sbuf_tensor("s_sb", [128, 2 * 512], F32))
    dsb = stageC2.enter_context(nc.sbuf_tensor("dsb", [128, N_HT], F32))

    p.dma("sync", dsb[:], d_ds, "dsc")
    p.wait("vector", "dsc", p.v("dsc"))

    def emit_wd_load(ug):
        if ug >= 4 * N_HT * 4:
            return
        ht, g4 = (ug // 4) % N_HT, ug % 4
        r = ug % 6
        p.wait("sync", "tV", last.get(("wdconv", r), 0))
        wld = p.dma("sync", wd_i[:, r * 2048:(r + 1) * 2048],
                    d_ld[ht, :, g4 * 2048:(g4 + 1) * 2048], "wd%d" % r)
        last[("wdld", r)] = wld

    for ug in range(5):
        emit_wd_load(ug)

    for tc in range(4):
        flush_st()
        p.wait("sync", "tP", last.get(("dmm_all",), 0))
        for itl in range(N_IT):
            hld = p.dma("sync", hh[:, itl * 512:(itl + 1) * 512],
                        s_h[itl, :, tc * 512:(tc + 1) * 512], "hh")
        p.wait("sync", "out", last.get(("outdma",), 0))
        for tsub in range(4):
            x2l = p.dma("sync", x2sb[:, tsub * 2048:(tsub + 1) * 2048],
                        d_xq[tc * 512 + tsub * 128: tc * 512 + (tsub + 1) * 128, :], "x2")
        hh_ch = p.v("hh")
        x2_ch = p.v("x2")
        for ht in range(N_HT):
            ypar = ht % 2
            yps = arena[:, ypar * 512:(ypar + 1) * 512]
            p.wait("tensor", "hh", hh_ch)
            p.wait("tensor", "tV", last.get(("sread", ypar), 0))
            for g4 in range(4):
                ug = (tc * N_HT + ht) * 4 + g4
                r = ug % 6
                emit_wd_load(ug + 5)
                p.wait("vector", "wd%d" % r, last[("wdld", r)])
                p.wait("vector", "tP", last.get(("wdmm", r), 0))
                p.op("vector", lambda e, s=r: e.tensor_scalar_add(
                    wdb[:, s * 2048:(s + 1) * 2048],
                    wd_i[:, s * 2048:(s + 1) * 2048], -128.0))
                cvt = p.inc_last("vector", "tV")
                last[("wdconv", r)] = cvt
                p.wait("tensor", "tV", cvt)
                for k in range(16):
                    it = g4 * 16 + k
                    lhs = wdb[:, r * 2048 + k * 128: r * 2048 + (k + 1) * 128]
                    p.op("tensor", lambda e, d=yps, l=lhs,
                         r=hh[:, it * 512:(it + 1) * 512],
                         st=(it == 0), sp=(it == 63): e.matmul(
                        d, l, r, start=st, stop=sp))
                tpd = p.inc_last("tensor", "tP")
                last[("wdmm", r)] = tpd
            last[("dmm_all",)] = tpd
            opar = ht % 2
            p.wait("sync", "tV", last.get(("oused", opar), 0))
            old = p.dma("sync", osb[:, opar * 512:(opar + 1) * 512],
                        s_o[ht, :, tc * 512:(tc + 1) * 512], "ob%d" % opar)
            spar = ht % 2
            p.wait("vector", "tP", tpd)
            p.wait("vector", "ob%d" % opar, old)
            p.wait("vector", "x2", x2_ch)
            p.wait("vector", "tP", last.get(("strans", spar), 0))
            p.op("vector", lambda e, y=yps, h=ht, sp=spar, op=opar: e.scalar_tensor_tensor(
                s_sb[:, sp * 512:(sp + 1) * 512], y, dsb[:, h:h + 1],
                osb[:, op * 512:(op + 1) * 512], ALU.mult, ALU.add))
            tvs = p.inc_last("vector", "tV")
            last[("oused", opar)] = tvs
            last[("sread", ypar)] = tvs
            tpar = ht % 2
            p.wait("tensor", "tV", tvs)
            p.wait("tensor", "tV", last.get(("outadd2", tpar), 0))
            for tsub in range(4):
                lhs = s_sb[:, spar * 512 + tsub * 128: spar * 512 + (tsub + 1) * 128]
                dst = arena[:, (2 + tpar) * 512 + tsub * 128:
                            (2 + tpar) * 512 + (tsub + 1) * 128]
                p.op("tensor", lambda e, d=dst, l=lhs: e.matmul(
                    d, l, g_idf[:], start=True, stop=True))
            tpt = p.inc_last("tensor", "tP")
            last[("strans", spar)] = tpt
            p.wait("vector", "tP", tpt)
            for tsub in range(4):
                src1 = arena[:, (2 + tpar) * 512 + tsub * 128:
                             (2 + tpar) * 512 + (tsub + 1) * 128]
                dst = x2sb[:, tsub * 2048 + ht * 128: tsub * 2048 + (ht + 1) * 128]
                p.op("vector", lambda e, d=dst, a=src1: e.tensor_tensor(
                    d, a, d, ALU.add))
            tva = p.inc_last("vector", "tV")
            last[("outadd2", tpar)] = tva
        outadd_tv = p.v("tV")

        def outst(tc=tc, outadd_tv=outadd_tv):
            p.wait("sync", "tV", outadd_tv)
            for tsub in range(4):
                dd = p.dma("sync",
                           d_out[tc * 512 + tsub * 128: tc * 512 + (tsub + 1) * 128, :],
                           x2sb[:, tsub * 2048:(tsub + 1) * 2048], "out")
            last[("outdma",)] = dd
        pend_st.append(outst)

    flush_st()
    p.barrier()

    with nc.Block() as block:
        @block.sync
        def _(eng):
            for fn in p.stream["sync"]:
                fn(eng)

        @block.tensor
        def _(eng):
            for fn in p.stream["tensor"]:
                fn(eng)

        @block.vector
        def _(eng):
            for fn in p.stream["vector"]:
                fn(eng)

        @block.scalar
        def _(eng):
            for fn in p.stream["scalar"]:
                fn(eng)

        @block.gpsimd
        def _(eng):
            for fn in p.stream["gpsimd"]:
                fn(eng)

    stageC2.close()
    stack.close()
    return nc


_NC_CACHE = {}


def _get_nc():
    if "nc" not in _NC_CACHE:
        _NC_CACHE["nc"] = build_nc()
    return _NC_CACHE["nc"]


def _host_prep(inputs):
    f32 = np.float32
    x = np.asarray(inputs["L0_attn_x"], f32)
    wq = np.ascontiguousarray(np.asarray(inputs["L0_attn_W_Q"], f32))
    wk = np.ascontiguousarray(np.asarray(inputs["L0_attn_W_K"], f32))
    wv = np.ascontiguousarray(np.asarray(inputs["L0_attn_W_V"], f32))
    wo = np.ascontiguousarray(np.asarray(inputs["L0_attn_W_O"], f32))
    fk0 = np.asarray(inputs["L0_attn_FK0"], f32)
    fv0 = np.asarray(inputs["L0_attn_FV0"], f32)
    fk1 = np.asarray(inputs["L0_attn_FK1"], f32)
    fv1 = np.asarray(inputs["L0_attn_FV1"], f32)
    pk0 = np.asarray(inputs["L0_attn_PK0"], f32)
    pv0 = np.asarray(inputs["L0_attn_PV0"], f32)
    pk1 = np.asarray(inputs["L0_attn_PK1"], f32)
    pv1 = np.asarray(inputs["L0_attn_PV1"], f32)
    gw = np.asarray(inputs["L0_ffn_gate_w"], np.int32)
    uw = np.asarray(inputs["L0_ffn_up_w"], np.int32)
    dw = np.asarray(inputs["L0_ffn_down_w"], np.int32)
    gs = np.asarray(inputs["L0_ffn_gate_scale"], f32)
    us = np.asarray(inputs["L0_ffn_up_scale"], f32)
    ds = np.asarray(inputs["L0_ffn_down_scale"], f32)

    fkv0 = np.ascontiguousarray(np.concatenate([fk0, fv0], axis=1))
    fkv1 = np.ascontiguousarray(np.concatenate([fk1, fv1], axis=1))

    def blockdiag(P):
        nh, r, dk = P.shape
        M = np.zeros((nh * r, nh * dk), f32)
        for h in range(nh):
            M[h * r:(h + 1) * r, h * dk:(h + 1) * dk] = P[h]
        return M

    pk0bd = blockdiag(pk0)
    pv0bd = blockdiag(pv0)
    pk1bd = blockdiag(pk1)
    pv1bd = blockdiag(pv1)

    gt = np.ascontiguousarray(gw.T)
    ut = np.ascontiguousarray(uw.T)
    dt = np.ascontiguousarray(dw.T)
    lg = np.ascontiguousarray(gt.reshape(16, 128, 64, 128).transpose(2, 1, 0, 3)
                              ).reshape(64, 128, 2048)
    lu = np.ascontiguousarray(ut.reshape(16, 128, 64, 128).transpose(2, 1, 0, 3)
                              ).reshape(64, 128, 2048)
    ld = np.ascontiguousarray(dt.reshape(64, 128, 16, 128).transpose(2, 1, 0, 3)
                              ).reshape(16, 128, 8192)
    gsc = np.ascontiguousarray(gs.reshape(64, 128).T)
    usc = np.ascontiguousarray(us.reshape(64, 128).T)
    dsc = np.ascontiguousarray(ds.reshape(16, 128).T)

    identb = np.eye(128, dtype=ml_dtypes.bfloat16)
    identf = np.eye(128, dtype=f32)
    ones_r = np.ones((1, 128), f32)

    masks = np.zeros((128, 4096), f32)
    qv = np.arange(512)
    for rt in range(4):
        kk = (rt * 128 + np.arange(128))[:, None]
        m = np.where(kk > qv[None, :], f32(-1e30), f32(0.0))
        masks[:, rt * 1024:rt * 1024 + 512] = m
        masks[:, rt * 1024 + 512:rt * 1024 + 1024] = m

    shared = dict(wq_f=wq, wk_f=wk, wv_f=wv, wo_f=wo, fkv0=fkv0, fkv1=fkv1,
                  pk0bd=pk0bd, pv0bd=pv0bd, pk1bd=pk1bd, pv1bd=pv1bd,
                  lg=lg, lu=lu, ld_t=ld, gsc_t=gsc, usc_t=usc, dsc_t=dsc,
                  identb=identb, identf=identf, ones_r=ones_r, masks=masks)

    in_maps = []
    qidx_all = []
    for c in range(8):
        seq = c // 4
        cp = c % 4
        xs = np.ascontiguousarray(x[seq])
        qpos = [512 * (4 * j + cp) for j in range(4)]
        qidx = np.concatenate([np.arange(q, q + 512) for q in qpos])
        xq = np.ascontiguousarray(xs[qidx])
        pad = np.zeros((NSLOT, 128, 64), f32)
        for j in range(4):
            nvalid = qpos[j] // 128
            pad[j, :, nvalid:] = -1e38
        m = dict(shared)
        m["x_full"] = xs
        m["x_q"] = xq
        m["padrow"] = pad
        in_maps.append(m)
        qidx_all.append((seq, qidx))
    return in_maps, qidx_all


def kernel(**inputs):
    in_maps, qidx_all = _host_prep(inputs)
    nc = _get_nc()
    res = run_bass_kernel_spmd(nc, in_maps, core_ids=list(range(8)))
    out = np.empty((2, SEQ, D), np.float32)
    for c in range(8):
        seq, qidx = qidx_all[c]
        out[seq, qidx] = np.asarray(res.results[c]["out_q"], np.float32)
    return out


# version 25
# speedup vs baseline: 1.0448x; 1.0448x over previous
import contextlib
import numpy as np
import ml_dtypes
import concourse.bass as bass
import concourse.mybir as mybir
from concourse.bass_utils import run_bass_kernel_spmd

F32 = mybir.dt.float32
BF16 = mybir.dt.bfloat16
I32 = mybir.dt.int32
AF = mybir.ActivationFunctionType
ALU = mybir.AluOpType
AX = mybir.AxisListType

SEQ = 8192
D = 2048
AQK = 1536
NQ = 2048
N_HT = 16
N_AT = 12
N_IT = 64
N_PAIR = 12
NSLOT = 4
MARGIN = 30.0
DEBUG_TAPS = False
GU_ENABLE = True


def qkv_ht_cols(ht):
    if ht < 2:
        return 0, 1536
    if ht < 8:
        return 256, 1280
    return 768, 768


def qkv_at_hts(at):
    if at < 2:
        return range(0, 2)
    if at < 6:
        return range(0, 8)
    return range(0, 16)


def wo_ht_ats(ht):
    if ht < 2:
        return range(0, 2)
    if ht < 8:
        return range(0, 6)
    return range(0, 12)


V_BANK_PLAN = [
    [(0, 512, [1]), (256, 256, list(range(2, 8)))],
    [(512, 512, [1]), (512, 512, list(range(2, 8))), (768, 256, list(range(8, 16)))],
    [(1024, 512, list(range(1, 16)))],
]


class Prog:
    ENGINES = ("sync", "tensor", "vector", "scalar", "gpsimd")

    def __init__(self, nc, stack):
        self.nc = nc
        self.stack = stack
        self.stream = {e: [] for e in self.ENGINES}
        self._sems = {}
        self.nbar = 0
        for s in ("tP", "tV", "tA", "tS", "dG"):
            h = stack.enter_context(nc.semaphore(s))
            self._sems[s] = [h, 0]

    def _sem(self, name):
        if name not in self._sems:
            h = self.stack.enter_context(self.nc.semaphore(name))
            self._sems[name] = [h, 0]
        return self._sems[name]

    def v(self, name):
        return self._sem(name)[1]

    def wait(self, eng, name, val=None):
        h, cur = self._sem(name)
        v = cur if val is None else val
        if v is None or v <= 0:
            return
        self.stream[eng].append(lambda e, h=h, v=v: e.wait_ge(h, v))

    def op(self, eng, fn):
        self.stream[eng].append(fn)

    def inc_last(self, eng, sem, n=1):
        rec = self._sem(sem)
        h = rec[0]
        fn = self.stream[eng].pop()
        self.stream[eng].append(lambda e, fn=fn, h=h, n=n: fn(e).then_inc(h, n))
        rec[1] += n
        return rec[1]

    def dma(self, eng, out, in_, ch="misc"):
        if eng == "gpsimd" and ch == "misc":
            ch = "dG"
        self.op(eng, lambda e, o=out, i=in_: e.dma_start(out=o, in_=i))
        return self.inc_last(eng, ch, 16)

    def barrier(self):
        for name, rec in list(self._sems.items()):
            if name in ("tP", "tV", "tA", "tS", "dG"):
                continue
            self.wait("sync", name, rec[1])
        self.op("sync", lambda e: e.sem_inc(self._sem("tS")[0], 1))
        self._sem("tS")[1] += 1
        self.wait("gpsimd", "dG", self.v("dG"))
        self.op("gpsimd", lambda e: e.sem_inc(self._sem("tS")[0], 1))
        self._sem("tS")[1] += 1
        ts = self.v("tS")
        tp, tv, ta = self.v("tP"), self.v("tV"), self.v("tA")
        for eng in self.ENGINES:
            self.wait(eng, "tS", ts)
            self.wait(eng, "tP", tp)
            self.wait(eng, "tV", tv)
            self.wait(eng, "tA", ta)


def build_nc():
    nc = bass.Bass("TRN2", target_bir_lowering=False, debug=False)

    d_x = nc.dram_tensor("x_full", [SEQ, D], F32, kind="ExternalInput").ap()
    d_xq = nc.dram_tensor("x_q", [NQ, D], F32, kind="ExternalInput").ap()
    d_wq = nc.dram_tensor("wq_f", [D, AQK], F32, kind="ExternalInput").ap()
    d_wk = nc.dram_tensor("wk_f", [D, AQK], F32, kind="ExternalInput").ap()
    d_wv = nc.dram_tensor("wv_f", [D, AQK], F32, kind="ExternalInput").ap()
    d_wo = nc.dram_tensor("wo_f", [AQK, D], F32, kind="ExternalInput").ap()
    d_fkv0 = nc.dram_tensor("fkv0", [1792, 64], F32, kind="ExternalInput").ap()
    d_fkv1 = nc.dram_tensor("fkv1", [1024, 128], F32, kind="ExternalInput").ap()
    d_pk0 = nc.dram_tensor("pk0bd", [32, 256], F32, kind="ExternalInput").ap()
    d_pv0 = nc.dram_tensor("pv0bd", [32, 256], F32, kind="ExternalInput").ap()
    d_pk1 = nc.dram_tensor("pk1bd", [64, 512], F32, kind="ExternalInput").ap()
    d_pv1 = nc.dram_tensor("pv1bd", [64, 512], F32, kind="ExternalInput").ap()
    d_lg = nc.dram_tensor("lg", [N_IT, 128, D], I32, kind="ExternalInput").ap()
    d_lu = nc.dram_tensor("lu", [N_IT, 128, D], I32, kind="ExternalInput").ap()
    d_ld = nc.dram_tensor("ld_t", [N_HT, 128, 8192], I32, kind="ExternalInput").ap()
    d_gs = nc.dram_tensor("gsc_t", [128, N_IT], F32, kind="ExternalInput").ap()
    d_us = nc.dram_tensor("usc_t", [128, N_IT], F32, kind="ExternalInput").ap()
    d_ds = nc.dram_tensor("dsc_t", [128, N_HT], F32, kind="ExternalInput").ap()
    d_idb = nc.dram_tensor("identb", [128, 128], BF16, kind="ExternalInput").ap()
    d_idf = nc.dram_tensor("identf", [128, 128], F32, kind="ExternalInput").ap()
    d_ones = nc.dram_tensor("ones_r", [1, 128], F32, kind="ExternalInput").ap()
    d_masks = nc.dram_tensor("masks", [128, 4096], F32, kind="ExternalInput").ap()
    d_pad = nc.dram_tensor("padrow", [NSLOT, 128, 64], F32, kind="ExternalInput").ap()

    d_out = nc.dram_tensor("out_q", [NQ, D], F32, kind="ExternalOutput").ap()

    KIND = "ExternalOutput" if DEBUG_TAPS else "Internal"
    s_k = nc.dram_tensor("k_scr", [AQK, SEQ], BF16, kind=KIND).ap()
    s_v = nc.dram_tensor("v_scr", [SEQ // 128, 128, 1920], BF16, kind=KIND).ap()
    s_kd = nc.dram_tensor("kd_scr", [AQK, NQ], BF16, kind=KIND).ap()
    s_vd = nc.dram_tensor("vd_scr", [NQ // 128, 128, 1920], BF16, kind=KIND).ap()
    s_xqf = nc.dram_tensor("xqf_scr", [N_HT, 128, NQ], BF16, kind=KIND).ap()
    s_at = nc.dram_tensor("at_scr", [AQK, NQ], BF16, kind=KIND).ap()
    s_o = nc.dram_tensor("o_scr", [N_HT, 128, NQ], F32, kind=KIND).ap()
    s_h = nc.dram_tensor("h_scr", [N_IT, 128, NQ], BF16, kind=KIND).ap()
    s_wkdump = (nc.dram_tensor("wk_dump", [2, 128, 16896], BF16,
                               kind="ExternalOutput").ap() if DEBUG_TAPS else None)

    stack = contextlib.ExitStack()
    p = Prog(nc, stack)
    last = {}

    g_idb = stack.enter_context(nc.sbuf_tensor("g_idb", [128, 128], BF16))
    g_idf = stack.enter_context(nc.sbuf_tensor("g_idf", [128, 128], F32))
    g_ones = stack.enter_context(nc.sbuf_tensor("g_ones", [1, 128], F32))
    g_eps = stack.enter_context(nc.sbuf_tensor("g_eps", [128, 1], F32))
    g_pad = stack.enter_context(nc.sbuf_tensor("g_pad", [128, NSLOT * 64], F32))

    arena = stack.enter_context(nc.psum_tensor("arena", [128, 4096], F32))

    def A(bank, cols=512, rows=128):
        return arena[0:rows, bank * 512: bank * 512 + cols]

    p.dma("sync", g_idb[:], d_idb, "cst")
    p.dma("sync", g_idf[:], d_idf, "cst")
    p.dma("sync", g_ones[:], d_ones, "cst")
    for j in range(NSLOT):
        p.dma("sync", g_pad[:, j * 64:j * 64 + 64], d_pad[j, :, :], "cst")
    consts_cst = p.v("cst")
    p.op("vector", lambda e: e.memset(g_eps[:], 1e-5))
    geps_tV = p.inc_last("vector", "tV")

    qkv_off = {}
    off = 0
    for ht in range(N_HT):
        a0, ncol = qkv_ht_cols(ht)
        qkv_off[ht] = (off, a0)
        off += ncol
    QKV_PACK = off


    qt_cm = nc.sbuf_tensor("qt_sb", [128, N_AT * NQ], BF16)
    qt_sb = qt_cm.__enter__()
    stageP = contextlib.ExitStack()
    xbuf = stageP.enter_context(nc.sbuf_tensor("xbuf", [128, 2 * D], F32))
    sq = stageP.enter_context(nc.sbuf_tensor("sq", [128, D], BF16))
    xnt = stageP.enter_context(nc.sbuf_tensor("xnt", [128, 4 * D], BF16))
    lnS = stageP.enter_context(nc.sbuf_tensor("lnS", [128, 6 * 140], F32))
    xnfm = stageP.enter_context(nc.sbuf_tensor("xnfm", [128, N_HT * 512], BF16))
    c0sb = stageP.enter_context(nc.sbuf_tensor("c0sb", [64, 512], BF16))
    c1sb = stageP.enter_context(nc.sbuf_tensor("c1sb", [128, 512], BF16))
    kstag = stageP.enter_context(nc.sbuf_tensor("kstag", [128, 2 * 512], BF16))
    vstag = stageP.enter_context(nc.sbuf_tensor("vstag", [128, 2 * 1920], BF16))


    for par in range(2):
        vv = vstag.ap().rearrange("p (r h d) -> p r h d", r=2, h=24)
        p.op("vector", lambda e, vv=vv, par=par: e.memset(vv[:, par, :, 64:80], 0.0))
        tvm = p.inc_last("vector", "tV")
        p.wait("vector", "tV", tvm)
        p.op("vector", lambda e, vv=vv, par=par: e.memset(vv[:, par, :, 64:65], 1.0))
        p.inc_last("vector", "tV")
    vstag_init_tV = p.v("tV")

    stagePA = contextlib.ExitStack()
    w_k = stagePA.enter_context(nc.sbuf_tensor("w_k", [128, QKV_PACK], BF16))
    w_v = stagePA.enter_context(nc.sbuf_tensor("w_v", [128, QKV_PACK], BF16))
    w_q = None
    w_f0 = stagePA.enter_context(nc.sbuf_tensor("w_f0", [128, 14 * 64], BF16))
    w_f1 = stagePA.enter_context(nc.sbuf_tensor("w_f1", [128, 8 * 128], BF16))
    w_pk0 = stagePA.enter_context(nc.sbuf_tensor("w_pk0", [32, 256], BF16))
    w_pv0 = stagePA.enter_context(nc.sbuf_tensor("w_pv0", [64, 256], BF16))
    w_pk1 = stagePA.enter_context(nc.sbuf_tensor("w_pk1", [64, 512], BF16))
    w_pv1 = stagePA.enter_context(nc.sbuf_tensor("w_pv1", [128, 512], BF16))

    def wslice(sb, ht, c0, ncols):
        o, a0 = qkv_off[ht]
        return sb[:, o + c0 - a0: o + c0 - a0 + ncols]

    for (sb, dr) in ((w_k, d_wk), (w_v, d_wv)):
        for ht in range(N_HT):
            o, a0 = qkv_off[ht]
            ncol = qkv_ht_cols(ht)[1]
            p.dma("gpsimd", sb[:, o:o + ncol], dr[ht * 128:(ht + 1) * 128, a0:a0 + ncol])
    for ht in range(14):
        p.dma("gpsimd", w_f0[:, ht * 64:(ht + 1) * 64], d_fkv0[ht * 128:(ht + 1) * 128, :])
    for ht in range(8):
        p.dma("gpsimd", w_f1[:, ht * 128:(ht + 1) * 128], d_fkv1[ht * 128:(ht + 1) * 128, :])
    p.dma("gpsimd", w_pk0[:], d_pk0)
    p.dma("gpsimd", w_pv0[32:64, :], d_pv0)
    p.dma("gpsimd", w_pk1[:], d_pk1)
    p.dma("gpsimd", w_pv1[64:128, :], d_pv1)
    weights_dG = p.v("dG")

    pend_st = []

    def flush_st():
        for f in pend_st:
            f()
        del pend_st[:]

    ln_n = [0]

    def ln_tile(sl, sl4):
        b = 6 * ln_n[0]
        ln_n[0] += 1
        sv, ss, rs, nm, nmrs, lnv = (lnS[:, b + i:b + i + 1] for i in range(6))
        xap = xbuf[:, sl * D:(sl + 1) * D]
        xnap = xnt[:, sl4 * D:(sl4 + 1) * D]
        p.op("scalar", lambda e: e.activation(sq[:], xap, AF.Copy, scale=-1.0,
                                              accum_out=sv))
        p.op("scalar", lambda e: e.drain())
        p.op("scalar", lambda e: e.activation(sq[:], xap, AF.Square, bias=sv,
                                              scale=float(D), accum_out=ss))
        p.op("scalar", lambda e: e.drain())
        p.op("scalar", lambda e: e.activation(lnv, ss, AF.Ln, bias=g_eps[:],
                                              scale=1.0 / float(D) ** 3))
        p.op("scalar", lambda e: e.drain())
        p.op("scalar", lambda e: e.activation(rs, lnv, AF.Exp, scale=-0.5))
        p.op("scalar", lambda e: e.activation(nm, sv, AF.Copy, scale=1.0 / D))
        p.op("scalar", lambda e: e.drain())
        p.op("scalar", lambda e: e.activation(nmrs, nm, AF.Copy, scale=rs))
        p.op("scalar", lambda e: e.drain())
        p.wait("scalar", "tP", last.get(("xnt_rd", sl4), 0))
        p.op("scalar", lambda e: e.activation(xnap, xap, AF.Identity,
                                              scale=rs, bias=nmrs))
        return p.inc_last("scalar", "tA")

    def proj_chunk(src, ci, mode):
        kdst = {"kv": s_k, "kvd": s_kd}.get(mode)
        vdst = {"kv": s_v, "kvd": s_vd}.get(mode)
        tok0 = ci * 512

        for tt in range(4):
            g = ci * 4 + tt
            sl = g % 2
            xld = p.dma("scalar", xbuf[:, sl * D:(sl + 1) * D],
                        src[tok0 + tt * 128: tok0 + (tt + 1) * 128, :], "x%d" % sl)
            p.wait("scalar", "x%d" % sl, xld)
            ta_ln = ln_tile(sl, tt)
            sb = (tt % 2) * 4
            p.wait("tensor", "tA", ta_ln)
            p.wait("tensor", "tV", last.get(("tevac", sb), 0))
            if sb == 0:
                p.wait("tensor", "tV", last.get(("slots03_rd", ), 0))
            else:
                p.wait("tensor", "tV", last.get(("slots47_rd", ), 0))
            for ht in range(N_HT):
                dst = arena[:, sb * 512 + ht * 128: sb * 512 + (ht + 1) * 128]
                lhs = xnt[:, tt * D + ht * 128: tt * D + (ht + 1) * 128]
                p.op("tensor", lambda e, d=dst, l=lhs: e.matmul(d, l, g_idb[:],
                                                                start=True, stop=True))
            tp = p.inc_last("tensor", "tP")
            last[("xnt_rd", tt)] = tp
            p.wait("vector", "tP", tp)
            p.wait("vector", "tP", last.get(("xnfm_free_tP", ), 0))
            p.wait("vector", "xqf", last.get(("xnfm_free_dS", ), 0))
            for s4 in range(4):
                src_ap = arena[:, (sb + s4) * 512:(sb + s4 + 1) * 512].rearrange(
                    "p (h t) -> p h t", h=4)
                dst_ap = xnfm.ap().rearrange("p (h t) -> p h t", h=N_HT)[
                    :, s4 * 4:(s4 + 1) * 4, tt * 128:(tt + 1) * 128]
                p.op("vector", lambda e, s=src_ap, d=dst_ap: e.tensor_copy(d, s))
            ta = p.inc_last("vector", "tV")
            last[("tevac", sb)] = ta
        xnfm_tA = p.v("tV")

        if mode != "q":
            p.wait("tensor", "tV", xnfm_tA)
            p.wait("tensor", "tV", last.get(("cevac", ), 0))
            for i, ht in enumerate(range(2, 16)):
                lhs = w_f0[:, (ht - 2) * 64:(ht - 1) * 64]
                rhs = xnfm[:, ht * 512:(ht + 1) * 512]
                p.op("tensor", lambda e, l=lhs, r=rhs, s=(i == 0): e.matmul(
                    A(0, 512, 64), l, r, start=s, stop=(ht == 15)))
            for i, ht in enumerate(range(8, 16)):
                lhs = w_f1[:, (ht - 8) * 128:(ht - 7) * 128]
                rhs = xnfm[:, ht * 512:(ht + 1) * 512]
                p.op("tensor", lambda e, l=lhs, r=rhs, s=(i == 0): e.matmul(
                    A(1, 512, 128), l, r, start=s, stop=(ht == 15)))
            tp = p.inc_last("tensor", "tP")
            p.wait("vector", "tP", tp)
            p.op("vector", lambda e: e.tensor_copy(c0sb[:], A(0, 512, 64)))
            p.op("vector", lambda e: e.tensor_copy(c1sb[:], A(1, 512, 128)))
            cev = p.inc_last("vector", "tV")
            last[("cevac", )] = cev
        else:
            cev = 0
            p.wait("tensor", "tV", xnfm_tA)

        wsb = w_q if mode == "q" else w_k
        p.wait("tensor", "tV", cev)
        for at in range(N_AT):
            slot = 2 + (at % 2)
            p.wait("tensor", "tV", last.get(("kevac", at % 2), 0))
            hts = list(qkv_at_hts(at))
            has_corr = (mode != "q") and (at < 6)
            for i, ht in enumerate(hts):
                lhs = wslice(wsb, ht, at * 128, 128)
                rhs = xnfm[:, ht * 512:(ht + 1) * 512]
                p.op("tensor", lambda e, d=A(slot), l=lhs, r=rhs,
                     s=(i == 0), st=(i == len(hts) - 1 and not has_corr): e.matmul(
                    d, l, r, start=s, stop=st))
            if has_corr:
                if at < 2:
                    lhs = w_pk0[:, at * 128:(at + 1) * 128]
                    p.op("tensor", lambda e, d=A(slot), l=lhs: e.matmul(
                        d, l, c0sb[0:32, :], start=False, stop=True))
                else:
                    lhs = w_pk1[:, (at - 2) * 128:(at - 1) * 128]
                    p.op("tensor", lambda e, d=A(slot), l=lhs: e.matmul(
                        d, l, c1sb[0:64, :], start=False, stop=True))
            tp = p.inc_last("tensor", "tP")
            p.wait("vector", "tP", tp)
            if mode == "q":
                dst = qt_sb[:, at * NQ + ci * 512: at * NQ + (ci + 1) * 512]
                p.op("vector", lambda e, d=dst, s=A(slot): e.tensor_copy(d, s))
                last[("kevac", at % 2)] = p.inc_last("vector", "tV")
            else:
                ks = kstag[:, (at % 2) * 512:(at % 2 + 1) * 512]
                p.wait("vector", "k%d" % (at % 2), last.get(("kdma", at % 2), 0))
                p.op("vector", lambda e, d=ks, s=A(slot): e.tensor_copy(d, s))
                tv = p.inc_last("vector", "tV")
                last[("kevac", at % 2)] = tv
                p.wait("sync", "tV", tv)
                dd = p.dma("sync", kdst[at * 128:(at + 1) * 128, ci * 512:(ci + 1) * 512],
                           ks, "k%d" % (at % 2))
                last[("kdma", at % 2)] = dd
        last[("slots03_rd", )] = p.v("tV")

        if mode == "q":
            p.wait("sync", "tV", xnfm_tA)
            for ht in range(N_HT):
                dd = p.dma("sync", s_xqf[ht, :, ci * 512:(ci + 1) * 512],
                           xnfm[:, ht * 512:(ht + 1) * 512], "xqf")
            last[("xnfm_free_dS", )] = dd
            last[("xnfm_free_tP", )] = p.v("tP")
            last[("slots47_rd", )] = p.v("tV")
            return

        vdst_scr = s_v if mode == "kv" else s_vd
        for tt in range(4):
            p.wait("tensor", "tV", last.get(("vevac", ), 0))
            for bank in range(3):
                lhsT = xnfm[:, 0 * 512 + tt * 128: 0 * 512 + tt * 128 + 128]
                rhs = wslice(w_v, 0, bank * 512, 512)
                p.op("tensor", lambda e, d=A(4 + bank), l=lhsT, r=rhs: e.matmul(
                    d, l, r, start=True, stop=False))
                for (c0, clen, hts) in V_BANK_PLAN[bank]:
                    for ht in hts:
                        lhsT = xnfm[:, ht * 512 + tt * 128: ht * 512 + tt * 128 + 128]
                        rhs = wslice(w_v, ht, c0, clen)
                        dst = arena[:, 4 * 512 + c0: 4 * 512 + c0 + clen]
                        p.op("tensor", lambda e, d=dst, l=lhsT, r=rhs: e.matmul(
                            d, l, r, start=False, stop=False))
            p.op("tensor", lambda e, l=c0sb[32:64, tt * 128:(tt + 1) * 128]: e.matmul(
                arena[:, 4 * 512: 4 * 512 + 256], l, w_pv0[32:64, :], start=False, stop=False))
            p.op("tensor", lambda e, l=c1sb[64:128, tt * 128:(tt + 1) * 128]: e.matmul(
                arena[:, 4 * 512 + 256: 4 * 512 + 768], l, w_pv1[64:128, :], start=False, stop=True))
            tp = p.inc_last("tensor", "tP")
            vsl = tt % 2
            p.wait("vector", "tP", tp)
            p.wait("vector", "v%d" % vsl, last.get(("vdma", vsl), 0))
            vsrc = arena[:, 4 * 512: 4 * 512 + 1536].rearrange("p (h d) -> p h d", h=24)
            vdst = vstag.ap().rearrange("p (r h d) -> p r h d", r=2, h=24)[:, vsl, :, 0:64]
            p.op("vector", lambda e, d=vdst, s=vsrc: e.tensor_copy(d, s))
            tv = p.inc_last("vector", "tV")
            last[("vevac", )] = tv
            p.wait("sync", "tV", tv)
            gt = ci * 4 + tt
            dd = p.dma("sync", vdst_scr[gt, :, :],
                       vstag[:, vsl * 1920:(vsl + 1) * 1920], "v%d" % vsl)
            last[("vdma", vsl)] = dd

        last[("xnfm_free_tP", )] = p.v("tP")
        last[("xnfm_free_dS", )] = 0
        last[("slots47_rd", )] = p.v("tV")

    p.wait("tensor", "dG", weights_dG)
    p.wait("tensor", "cst", consts_cst)
    p.wait("tensor", "tV", vstag_init_tV)
    p.wait("scalar", "cst", consts_cst)
    p.wait("scalar", "tV", geps_tV)
    if DEBUG_TAPS:
        p.wait("sync", "dG", weights_dG)
        p.dma("sync", s_wkdump[0, :, :], w_k[:], "wkd")
        p.dma("sync", s_wkdump[1, :, :], w_v[:], "wkd")

    for ci in range(16):
        proj_chunk(d_x, ci, "kv")
    for ci in range(4):
        proj_chunk(d_xq, ci, "kvd")

    stagePA.close()
    stagePB = contextlib.ExitStack()
    w_q = stagePB.enter_context(nc.sbuf_tensor("w_q", [128, QKV_PACK], BF16))
    p.wait("gpsimd", "tP", p.v("tP"))
    for ht in range(N_HT):
        o, a0 = qkv_off[ht]
        ncol = qkv_ht_cols(ht)[1]
        p.dma("gpsimd", w_q[:, o:o + ncol], d_wq[ht * 128:(ht + 1) * 128, a0:a0 + ncol])
    p.wait("tensor", "dG", p.v("dG"))
    for ci in range(4):
        proj_chunk(d_xq, ci, "q")
    stagePB.close()
    stageP.close()


    p.barrier()

    stage3 = contextlib.ExitStack()
    masks_sb = stage3.enter_context(nc.sbuf_tensor("masks_sb", [128, 4096], F32))
    kbuf = stage3.enter_context(nc.sbuf_tensor("kbuf", [128, 2 * 256], BF16))
    vbuf = stage3.enter_context(nc.sbuf_tensor("vbuf", [128, 2 * 320], BF16))
    ubuf = stage3.enter_context(nc.sbuf_tensor("ubuf", [128, 4 * 1024], BF16))
    bb = stage3.enter_context(nc.sbuf_tensor("bb", [128, 64 * NSLOT * N_PAIR], F32))
    cb = stage3.enter_context(nc.sbuf_tensor("cb", [128, N_PAIR], F32))
    smax = stage3.enter_context(nc.sbuf_tensor("smax", [128, N_PAIR], F32))
    gmx = stage3.enter_context(nc.sbuf_tensor("gmx", [1, 2 * N_PAIR], F32))
    rc = stage3.enter_context(nc.sbuf_tensor("rc", [1, 512], F32))
    bcsb = stage3.enter_context(nc.sbuf_tensor("bcsb", [64, 512], F32))
    at_sb = stage3.enter_context(nc.sbuf_tensor("at_sb", [64, 2 * 512], BF16))
    xq_all = stage3.enter_context(nc.sbuf_tensor("xq_all", [128, N_HT * 2048], BF16))
    wgb = stage3.enter_context(nc.sbuf_tensor("wgb", [128, 2 * 2048], BF16))
    wub = stage3.enter_context(nc.sbuf_tensor("wub", [128, 2 * 2048], BF16))
    t1f = stage3.enter_context(nc.sbuf_tensor("t1f", [128, 2 * 512], F32))
    hbb = stage3.enter_context(nc.sbuf_tensor("hbb", [128, 2 * 512], BF16))
    gsb = stage3.enter_context(nc.sbuf_tensor("gsb", [128, N_IT], F32))
    usb = stage3.enter_context(nc.sbuf_tensor("usb", [128, N_IT], F32))

    p.dma("sync", masks_sb[:], d_masks, "msk")
    masks_ch = p.v("msk")
    p.wait("scalar", "msk", masks_ch)
    p.wait("vector", "msk", masks_ch)
    p.dma("sync", gsb[:], d_gs, "xqa")
    p.dma("sync", usb[:], d_us, "xqa")
    for ht in range(N_HT):
        p.dma("sync", xq_all[:, ht * 2048:(ht + 1) * 2048], s_xqf[ht, :, :], "xqa")
    xqa_ch = p.v("xqa")
    p.wait("tensor", "xqa", xqa_ch)
    p.wait("scalar", "xqa", xqa_ch)
    p.wait("vector", "xqa", xqa_ch)

    hcnt = [0]

    def emit_gu_load(it):
        if it >= N_IT:
            return
        sl = it % 2
        p.wait("gpsimd", "tV", last.get(("guconv", sl), 0))
        p.wait("gpsimd", "tP", last.get(("gumm", sl), 0))
        p.dma("gpsimd", wgb[:, sl * 2048:(sl + 1) * 2048], d_lg[it, :, :], "lw%d" % sl)
        uld = p.dma("gpsimd", wub[:, sl * 2048:(sl + 1) * 2048], d_lu[it, :, :], "lw%d" % sl)
        last[("guld", sl)] = uld

    def emit_gu_conv(it):
        sl = it % 2
        p.wait("vector", "lw%d" % sl, last[("guld", sl)])
        p.wait("vector", "tP", last.get(("gumm", sl), 0))
        p.op("vector", lambda e, s=sl: e.tensor_scalar_add(
            wgb[:, s * 2048:(s + 1) * 2048], wgb[:, s * 2048:(s + 1) * 2048], -128.0))
        p.op("vector", lambda e, s=sl: e.tensor_scalar_add(
            wub[:, s * 2048:(s + 1) * 2048], wub[:, s * 2048:(s + 1) * 2048], -128.0))
        cvt = p.inc_last("vector", "tV")
        last[("guconv", sl)] = cvt

    def gu_quanta():
        emit_gu_load(0)
        emit_gu_conv(0)
        emit_gu_load(1)
        for it in range(N_IT):
            sl = it % 2
            for tc in range(4):
                tsl = tc % 2
                def q(it=it, sl=sl, tc=tc, tsl=tsl):
                    flush_st()
                    p.wait("tensor", "tV", last.get(("guconv", sl), 0))
                    p.wait("tensor", "tA", last.get(("b7silu",), 0))
                    for ht in range(N_HT):
                        lhs = wgb[:, sl * 2048 + ht * 128: sl * 2048 + (ht + 1) * 128]
                        rhs = xq_all[:, ht * 2048 + tc * 512: ht * 2048 + (tc + 1) * 512]
                        p.op("tensor", lambda e, d=arena[:, 7 * 512:8 * 512],
                             l=lhs, r=rhs, st=(ht == 0), sp=(ht == 15): e.matmul(
                            d, l, r, start=st, stop=sp))
                    tg = p.inc_last("tensor", "tP")
                    p.wait("tensor", "tV", last.get(("b6sts",), 0))
                    p.wait("tensor", "tA", last.get(("b6a",), 0))
                    p.wait("tensor", "tV", last.get(("b6v",), 0))
                    for ht in range(N_HT):
                        lhs = wub[:, sl * 2048 + ht * 128: sl * 2048 + (ht + 1) * 128]
                        rhs = xq_all[:, ht * 2048 + tc * 512: ht * 2048 + (tc + 1) * 512]
                        p.op("tensor", lambda e, d=arena[:, 6 * 512:7 * 512],
                             l=lhs, r=rhs, st=(ht == 0), sp=(ht == 15): e.matmul(
                            d, l, r, start=st, stop=sp))
                    tu = p.inc_last("tensor", "tP")
                    last[("gumm", sl)] = tu
                    p.wait("scalar", "tP", tg)
                    p.wait("scalar", "tV", last.get(("t1rd", tsl), 0))
                    p.op("scalar", lambda e: e.activation(
                        t1f[:, tsl * 512:(tsl + 1) * 512],
                        arena[:, 7 * 512:8 * 512], AF.Silu,
                        scale=gsb[:, it:it + 1]))
                    last[("b7silu",)] = p.inc_last("scalar", "tA")
                    p.wait("vector", "tP", tu)
                    p.wait("vector", "tA", last[("b7silu",)])
                    p.wait("vector", "h%d" % tsl, last.get(("hdma", tsl), 0))
                    p.op("vector", lambda e: e.scalar_tensor_tensor(
                        hbb[:, tsl * 512:(tsl + 1) * 512],
                        arena[:, 6 * 512:7 * 512], usb[:, it:it + 1],
                        t1f[:, tsl * 512:(tsl + 1) * 512], ALU.mult, ALU.mult))
                    tvh = p.inc_last("vector", "tV")
                    last[("b6sts",)] = tvh
                    last[("t1rd", tsl)] = tvh

                    def st(it=it, tc=tc, tsl=tsl, tvh=tvh):
                        p.wait("sync", "tV", tvh)
                        dd = p.dma("sync", s_h[it, :, tc * 512:(tc + 1) * 512],
                                   hbb[:, tsl * 512:(tsl + 1) * 512], "h%d" % tsl)
                        last[("hdma", tsl)] = dd
                    pend_st.append(st)
                    if tc == 3:
                        if it + 1 < N_IT:
                            emit_gu_conv(it + 1)
                        emit_gu_load(it + 2)
                yield q
    gu_gen = gu_quanta()
    gu_left = [bool(GU_ENABLE)]
    gu_slot = [0]

    def emit_gu_quantum():
        if not gu_left[0]:
            return
        try:
            next(gu_gen)()
        except StopIteration:
            gu_left[0] = False

    at_dma_n = 0

    for hp in range(N_PAIR):
        cbA = cb[:, hp:hp + 1]
        for j in range(NSLOT):
            bbA = bb[:, (hp * NSLOT + j) * 64:(hp * NSLOT + j) * 64 + 64]
            nch = 16 * j + 12
            tp_av = last.get(("avP",), 0)
            pend_av = []
            for ch in range(nch + 4):
                diag = ch >= nch
                chd = ch - nch
                par = ch % 2
                usl = ch % 4
                Sap = arena[:, par * 1024:(par + 1) * 1024]
                if ch % 2 == 0:
                    kl = ch // 2
                    ksl = kl % 2
                    p.wait("sync", "tP", last.get(("smm", ksl), 0))
                    if not diag:
                        ksrc = s_k[hp * 128:(hp + 1) * 128, kl * 256:(kl + 1) * 256]
                    else:
                        ksrc = s_kd[hp * 128:(hp + 1) * 128,
                                    j * 512 + (chd // 2) * 256:j * 512 + (chd // 2 + 1) * 256]
                    kld = p.dma("sync", kbuf[:, ksl * 256:(ksl + 1) * 256], ksrc,
                                "kb%d" % ksl)
                    last[("kld", ksl)] = kld
                    p.wait("sync", "tP", last.get(("avv", ksl), 0))
                    vsc = s_v if not diag else s_vd
                    tt0 = (2 * kl) if not diag else (4 * j + 2 * (chd // 2))
                    vsrc = vsc[tt0:tt0 + 2, :, hp * 160:(hp + 1) * 160].rearrange(
                        "t p d -> p t d")
                    vd3 = vbuf.ap().rearrange("p (r t d) -> p r t d", r=2, t=2)[:, ksl]
                    vld = p.dma("sync", vd3, vsrc, "vb%d" % ksl)
                    last[("vld", ksl)] = vld
                kl = ch // 2
                ksl = kl % 2
                kap = kbuf[:, ksl * 256 + (ch % 2) * 128: ksl * 256 + (ch % 2) * 128 + 128]
                p.wait("tensor", "tA", last.get(("exp", par), 0))
                p.wait("tensor", "kb%d" % ksl, last[("kld", ksl)])
                for h2 in (0, 1):
                    lhs = kap[h2 * 64:(h2 + 1) * 64, :] if False else None
                    p.op("tensor", lambda e,
                         l=kbuf[h2 * 64:(h2 + 1) * 64,
                                ksl * 256 + (ch % 2) * 128: ksl * 256 + (ch % 2) * 128 + 128],
                         r=qt_sb[h2 * 64:(h2 + 1) * 64,
                                 hp * NQ + j * 512: hp * NQ + (j + 1) * 512],
                         d=arena[:, par * 1024 + h2 * 512: par * 1024 + (h2 + 1) * 512],
                         tp=((64, 0) if h2 else None): e.matmul(
                        d, l, r, start=True, stop=True, tile_position=tp))
                tp_s = p.inc_last("tensor", "tP")
                last[("smm", ksl)] = tp_s

                if j == 0 and ch == 0:
                    smx = smax[:, hp:hp + 1]
                    gmx0 = gmx[0:1, 2 * hp:2 * hp + 1]
                    gmx1 = gmx[0:1, 2 * hp + 1:2 * hp + 2]
                    p.wait("vector", "tP", tp_s)
                    p.op("vector", lambda e, o=smx: e.reduce_max(
                        o, arena[:, 0:1024:4], axis=AX.X))
                    p.op("vector", lambda e: e.drain())
                    tv = p.inc_last("vector", "tV")
                    p.wait("tensor", "tV", tv)
                    p.wait("tensor", "tA", last.get(("b6a",), 0))
                    p.wait("tensor", "tV", last.get(("b6v",), 0))
                    p.wait("tensor", "tV", last.get(("b6sts",), 0))
                    p.op("tensor", lambda e, l=smx: e.matmul(
                        arena[0:1, 6 * 512:6 * 512 + 128], l, g_idf[:],
                        start=True, stop=True))
                    tp2 = p.inc_last("tensor", "tP")
                    p.wait("vector", "tP", tp2)
                    p.op("vector", lambda e, o=gmx0: e.reduce_max(
                        o, arena[0:1, 6 * 512:6 * 512 + 128], axis=AX.X))
                    p.op("vector", lambda e: e.drain())
                    p.op("vector", lambda e, o=gmx1, i=gmx0: e.tensor_scalar(
                        o, i, -0.125, -MARGIN, ALU.mult, ALU.add))
                    p.op("vector", lambda e: e.drain())
                    tv2 = p.inc_last("vector", "tV")
                    last[("b6v",)] = tv2
                    p.wait("tensor", "tV", tv2)
                    p.op("tensor", lambda e, r=gmx1: e.matmul(
                        arena[0:128, 6 * 512:6 * 512 + 1], g_ones[:, :],
                        r, start=True, stop=True))
                    tp3 = p.inc_last("tensor", "tP")
                    p.wait("scalar", "tP", tp3)
                    p.op("scalar", lambda e, o=cbA: e.copy(o, arena[0:128, 6 * 512:6 * 512 + 1]))
                    cbA_tA = p.inc_last("scalar", "tA")
                    last[("cbA",)] = cbA_tA
                    last[("b6a",)] = cbA_tA
                    p.wait("vector", "tA", cbA_tA)
                    for jj in range(NSLOT):
                        p.op("vector", lambda e, o=bb[:, (hp * NSLOT + jj) * 64:
                                                      (hp * NSLOT + jj) * 64 + 64],
                             c=cbA, jx=jj: e.tensor_scalar(
                            o, g_pad[:, jx * 64:(jx + 1) * 64], c, None, ALU.add))
                    p.op("vector", lambda e: e.drain())
                    last[("bbV", hp)] = p.inc_last("vector", "tV")

                if diag:
                    p.wait("vector", "tP", tp_s)
                    p.op("vector", lambda e, cd=chd, S=Sap: e.tensor_tensor(
                        S, S, masks_sb[:, cd * 1024:(cd + 1) * 1024], ALU.add))
                    last[("mskV",)] = p.inc_last("vector", "tV")

                p.wait("scalar", "tP", tp_s)
                if diag:
                    p.wait("scalar", "tV", last[("mskV",)])
                p.wait("scalar", "tV", last.get(("bbV", hp), 0))
                p.wait("scalar", "tP", last.get(("avu", usl), 0))
                bias_ap = cbA if diag else bbA[:, ch:ch + 1]
                p.op("scalar", lambda e, b=bias_ap, u=usl, S=Sap: e.activation(
                    ubuf[:, u * 1024:(u + 1) * 1024], S, AF.Exp,
                    bias=b, scale=0.125))
                last[("exp", par)] = p.inc_last("scalar", "tA")

                def emit_av(ch=ch, ksl=ksl, usl=usl, exp_ta=last[("exp", par)],
                            vld=last[("vld", ksl)], is_last=(ch == nch + 3),
                            avrd=(last.get(("avrd",), 0) if ch == 0 else 0)):
                    p.wait("tensor", "tA", exp_ta)
                    p.wait("tensor", "vb%d" % ksl, vld)
                    if ch == 0:
                        p.wait("tensor", "tV", avrd)
                    for h2 in (0, 1):
                        lhs = vbuf[:, ksl * 320 + (ch % 2) * 160 + h2 * 80:
                                   ksl * 320 + (ch % 2) * 160 + h2 * 80 + 65]
                        rhs = ubuf[:, usl * 1024 + h2 * 512: usl * 1024 + (h2 + 1) * 512]
                        dst = arena[0:65, (4 + h2) * 512:(4 + h2) * 512 + 512]
                        p.op("tensor", lambda e, d=dst, l=lhs, r=rhs,
                             st=(ch == 0), sp=(is_last and h2 == 1): e.matmul(
                            d, l, r, start=st, stop=sp))
                    t = p.inc_last("tensor", "tP")
                    last[("avu", usl)] = t
                    last[("avv", ksl)] = t
                    return t
                pend_av.append(emit_av)
                while len(pend_av) > 2:
                    tp_av = pend_av.pop(0)()
                if ch % 2 == 1:
                    gu_slot[0] += 1
                    if gu_slot[0] % 6 == 1:
                        emit_gu_quantum()
            while pend_av:
                tp_av = pend_av.pop(0)()
            last[("avP",)] = tp_av
            for h2 in (0, 1):
                flush_st()
                avp = arena[0:65, (4 + h2) * 512:(4 + h2) * 512 + 512]
                p.wait("vector", "tP", tp_av)
                p.op("vector", lambda e, a=avp: e.reciprocal(rc[:], a[64:65, :]))
                tvr = p.inc_last("vector", "tV")
                emit_gu_quantum()
                p.wait("tensor", "tV", tvr)
                p.wait("tensor", "tA", last.get(("b6a",), 0))
                p.wait("tensor", "tV", last.get(("b6sts",), 0))
                p.op("tensor", lambda e: e.matmul(
                    arena[0:64, 6 * 512:6 * 512 + 512], g_ones[0:1, 0:64],
                    rc[0:1, :], start=True, stop=True))
                tpb = p.inc_last("tensor", "tP")
                p.wait("scalar", "tP", tpb)
                p.op("scalar", lambda e: e.copy(bcsb[:], arena[0:64, 6 * 512:6 * 512 + 512]))
                tab = p.inc_last("scalar", "tA")
                last[("b6a",)] = tab
                asl = at_dma_n % 2
                p.wait("vector", "tA", tab)
                p.wait("vector", "at%d" % asl, last.get(("atdma", asl), 0))
                p.op("vector", lambda e, a=avp, sl=asl: e.tensor_tensor(
                    at_sb[:, sl * 512:(sl + 1) * 512], a[0:64, :], bcsb[:], ALU.mult))
                tvn = p.inc_last("vector", "tV")
                head = 2 * hp + h2

                def atst(head=head, j=j, asl=asl, tvn=tvn):
                    p.wait("sync", "tV", tvn)
                    dd = p.dma("sync", s_at[head * 64:(head + 1) * 64,
                                            j * 512:(j + 1) * 512],
                               at_sb[:, asl * 512:(asl + 1) * 512], "at%d" % asl)
                    last[("atdma", asl)] = dd
                pend_st.append(atst)
                at_dma_n += 1
            last[("avrd",)] = p.v("tV")

    while gu_left[0]:
        emit_gu_quantum()
    flush_st()

    stage3.close()
    qt_cm.__exit__(None, None, None)
    p.barrier()

    stageC1 = contextlib.ExitStack()

    wo_off = {}
    off = 0
    for at in range(N_AT):
        if at < 2:
            h0, nc_ = 0, 2048
        elif at < 6:
            h0, nc_ = 256, 1792
        else:
            h0, nc_ = 1024, 1024
        wo_off[at] = (off, h0)
        off += nc_
    WO_PACK = off
    wo_sb = stageC1.enter_context(nc.sbuf_tensor("wo_sb", [128, WO_PACK], BF16))
    atc = stageC1.enter_context(nc.sbuf_tensor("atc", [128, N_AT * 512], BF16))
    ostg = stageC1.enter_context(nc.sbuf_tensor("ostg", [128, 2 * 512], F32))

    for at in range(N_AT):
        o, h0 = wo_off[at]
        nc_ = {0: 2048, 1: 2048}.get(at, 1792 if at < 6 else 1024)
        p.dma("gpsimd", wo_sb[:, o:o + nc_], d_wo[at * 128:(at + 1) * 128, h0:h0 + nc_])
    wo_dG = p.v("dG")
    p.wait("tensor", "dG", wo_dG)

    for tc in range(4):
        p.wait("sync", "tP", last.get(("oproj_mm",), 0))
        for at in range(N_AT):
            atl = p.dma("sync", atc[:, at * 512:(at + 1) * 512],
                        s_at[at * 128:(at + 1) * 128, tc * 512:(tc + 1) * 512], "atc")
        p.wait("tensor", "atc", atl)
        for ht in range(N_HT):
            flush_st()
            par = ht % 2
            ats = list(wo_ht_ats(ht))
            p.wait("tensor", "tV", last.get(("oev", par), 0))
            for i, at in enumerate(ats):
                o, h0 = wo_off[at]
                lhs = wo_sb[:, o + ht * 128 - h0: o + ht * 128 - h0 + 128]
                p.op("tensor", lambda e, d=arena[:, par * 512:(par + 1) * 512],
                     l=lhs, r=atc[:, at * 512:(at + 1) * 512],
                     st=(i == 0), sp=(i == len(ats) - 1): e.matmul(
                    d, l, r, start=st, stop=sp))
            tpo = p.inc_last("tensor", "tP")
            last[("oproj_mm",)] = tpo
            p.wait("vector", "tP", tpo)
            p.wait("vector", "o%d" % par, last.get(("odma", par), 0))
            p.op("vector", lambda e, pr=par: e.tensor_copy(
                ostg[:, pr * 512:(pr + 1) * 512], arena[:, pr * 512:(pr + 1) * 512]))
            tvo = p.inc_last("vector", "tV")
            last[("oev", par)] = tvo

            def ost(ht=ht, tc=tc, par=par, tvo=tvo):
                p.wait("sync", "tV", tvo)
                dd = p.dma("sync", s_o[ht, :, tc * 512:(tc + 1) * 512],
                           ostg[:, par * 512:(par + 1) * 512], "o%d" % par)
                last[("odma", par)] = dd
            pend_st.append(ost)

    flush_st()
    stageC1.close()
    p.barrier()

    stageC2 = contextlib.ExitStack()
    hh = stageC2.enter_context(nc.sbuf_tensor("hh", [128, N_IT * 512], BF16))
    wd_i = stageC2.enter_context(nc.sbuf_tensor("wd_i", [128, 6 * 2048], I32))
    wdb = stageC2.enter_context(nc.sbuf_tensor("wdb", [128, 6 * 2048], BF16))
    x2sb = stageC2.enter_context(nc.sbuf_tensor("x2sb", [128, 4 * 2048], F32))
    osb = stageC2.enter_context(nc.sbuf_tensor("osb", [128, 2 * 512], F32))
    s_sb = stageC2.enter_context(nc.sbuf_tensor("s_sb", [128, 2 * 512], F32))
    dsb = stageC2.enter_context(nc.sbuf_tensor("dsb", [128, N_HT], F32))

    p.dma("sync", dsb[:], d_ds, "dsc")
    p.wait("vector", "dsc", p.v("dsc"))

    def emit_wd_load(ug):
        if ug >= 4 * N_HT * 4:
            return
        ht, g4 = (ug // 4) % N_HT, ug % 4
        r = ug % 6
        p.wait("sync", "tV", last.get(("wdconv", r), 0))
        wld = p.dma("sync", wd_i[:, r * 2048:(r + 1) * 2048],
                    d_ld[ht, :, g4 * 2048:(g4 + 1) * 2048], "wd%d" % r)
        last[("wdld", r)] = wld

    for ug in range(5):
        emit_wd_load(ug)

    for tc in range(4):
        flush_st()
        p.wait("sync", "tP", last.get(("dmm_all",), 0))
        for itl in range(N_IT):
            hld = p.dma("sync", hh[:, itl * 512:(itl + 1) * 512],
                        s_h[itl, :, tc * 512:(tc + 1) * 512], "hh")
        p.wait("sync", "out", last.get(("outdma",), 0))
        for tsub in range(4):
            x2l = p.dma("sync", x2sb[:, tsub * 2048:(tsub + 1) * 2048],
                        d_xq[tc * 512 + tsub * 128: tc * 512 + (tsub + 1) * 128, :], "x2")
        hh_ch = p.v("hh")
        x2_ch = p.v("x2")
        for ht in range(N_HT):
            ypar = ht % 2
            yps = arena[:, ypar * 512:(ypar + 1) * 512]
            p.wait("tensor", "hh", hh_ch)
            p.wait("tensor", "tV", last.get(("sread", ypar), 0))
            for g4 in range(4):
                ug = (tc * N_HT + ht) * 4 + g4
                r = ug % 6
                emit_wd_load(ug + 5)
                p.wait("vector", "wd%d" % r, last[("wdld", r)])
                p.wait("vector", "tP", last.get(("wdmm", r), 0))
                p.op("vector", lambda e, s=r: e.tensor_scalar_add(
                    wdb[:, s * 2048:(s + 1) * 2048],
                    wd_i[:, s * 2048:(s + 1) * 2048], -128.0))
                cvt = p.inc_last("vector", "tV")
                last[("wdconv", r)] = cvt
                p.wait("tensor", "tV", cvt)
                for k in range(16):
                    it = g4 * 16 + k
                    lhs = wdb[:, r * 2048 + k * 128: r * 2048 + (k + 1) * 128]
                    p.op("tensor", lambda e, d=yps, l=lhs,
                         r=hh[:, it * 512:(it + 1) * 512],
                         st=(it == 0), sp=(it == 63): e.matmul(
                        d, l, r, start=st, stop=sp))
                tpd = p.inc_last("tensor", "tP")
                last[("wdmm", r)] = tpd
            last[("dmm_all",)] = tpd
            opar = ht % 2
            p.wait("sync", "tV", last.get(("oused", opar), 0))
            old = p.dma("sync", osb[:, opar * 512:(opar + 1) * 512],
                        s_o[ht, :, tc * 512:(tc + 1) * 512], "ob%d" % opar)
            spar = ht % 2
            p.wait("vector", "tP", tpd)
            p.wait("vector", "ob%d" % opar, old)
            p.wait("vector", "x2", x2_ch)
            p.wait("vector", "tP", last.get(("strans", spar), 0))
            p.op("vector", lambda e, y=yps, h=ht, sp=spar, op=opar: e.scalar_tensor_tensor(
                s_sb[:, sp * 512:(sp + 1) * 512], y, dsb[:, h:h + 1],
                osb[:, op * 512:(op + 1) * 512], ALU.mult, ALU.add))
            tvs = p.inc_last("vector", "tV")
            last[("oused", opar)] = tvs
            last[("sread", ypar)] = tvs
            tpar = ht % 2
            p.wait("tensor", "tV", tvs)
            p.wait("tensor", "tV", last.get(("outadd2", tpar), 0))
            for tsub in range(4):
                lhs = s_sb[:, spar * 512 + tsub * 128: spar * 512 + (tsub + 1) * 128]
                dst = arena[:, (2 + tpar) * 512 + tsub * 128:
                            (2 + tpar) * 512 + (tsub + 1) * 128]
                p.op("tensor", lambda e, d=dst, l=lhs: e.matmul(
                    d, l, g_idf[:], start=True, stop=True))
            tpt = p.inc_last("tensor", "tP")
            last[("strans", spar)] = tpt
            p.wait("vector", "tP", tpt)
            for tsub in range(4):
                src1 = arena[:, (2 + tpar) * 512 + tsub * 128:
                             (2 + tpar) * 512 + (tsub + 1) * 128]
                dst = x2sb[:, tsub * 2048 + ht * 128: tsub * 2048 + (ht + 1) * 128]
                p.op("vector", lambda e, d=dst, a=src1: e.tensor_tensor(
                    d, a, d, ALU.add))
            tva = p.inc_last("vector", "tV")
            last[("outadd2", tpar)] = tva
        outadd_tv = p.v("tV")

        def outst(tc=tc, outadd_tv=outadd_tv):
            p.wait("sync", "tV", outadd_tv)
            for tsub in range(4):
                dd = p.dma("sync",
                           d_out[tc * 512 + tsub * 128: tc * 512 + (tsub + 1) * 128, :],
                           x2sb[:, tsub * 2048:(tsub + 1) * 2048], "out")
            last[("outdma",)] = dd
        pend_st.append(outst)

    flush_st()
    p.barrier()

    with nc.Block() as block:
        @block.sync
        def _(eng):
            for fn in p.stream["sync"]:
                fn(eng)

        @block.tensor
        def _(eng):
            for fn in p.stream["tensor"]:
                fn(eng)

        @block.vector
        def _(eng):
            for fn in p.stream["vector"]:
                fn(eng)

        @block.scalar
        def _(eng):
            for fn in p.stream["scalar"]:
                fn(eng)

        @block.gpsimd
        def _(eng):
            for fn in p.stream["gpsimd"]:
                fn(eng)

    stageC2.close()
    stack.close()
    return nc


_NC_CACHE = {}


def _get_nc():
    if "nc" not in _NC_CACHE:
        _NC_CACHE["nc"] = build_nc()
    return _NC_CACHE["nc"]


def _host_prep(inputs):
    f32 = np.float32
    x = np.asarray(inputs["L0_attn_x"], f32)
    wq = np.ascontiguousarray(np.asarray(inputs["L0_attn_W_Q"], f32))
    wk = np.ascontiguousarray(np.asarray(inputs["L0_attn_W_K"], f32))
    wv = np.ascontiguousarray(np.asarray(inputs["L0_attn_W_V"], f32))
    wo = np.ascontiguousarray(np.asarray(inputs["L0_attn_W_O"], f32))
    fk0 = np.asarray(inputs["L0_attn_FK0"], f32)
    fv0 = np.asarray(inputs["L0_attn_FV0"], f32)
    fk1 = np.asarray(inputs["L0_attn_FK1"], f32)
    fv1 = np.asarray(inputs["L0_attn_FV1"], f32)
    pk0 = np.asarray(inputs["L0_attn_PK0"], f32)
    pv0 = np.asarray(inputs["L0_attn_PV0"], f32)
    pk1 = np.asarray(inputs["L0_attn_PK1"], f32)
    pv1 = np.asarray(inputs["L0_attn_PV1"], f32)
    gw = np.asarray(inputs["L0_ffn_gate_w"], np.int32)
    uw = np.asarray(inputs["L0_ffn_up_w"], np.int32)
    dw = np.asarray(inputs["L0_ffn_down_w"], np.int32)
    gs = np.asarray(inputs["L0_ffn_gate_scale"], f32)
    us = np.asarray(inputs["L0_ffn_up_scale"], f32)
    ds = np.asarray(inputs["L0_ffn_down_scale"], f32)

    fkv0 = np.ascontiguousarray(np.concatenate([fk0, fv0], axis=1))
    fkv1 = np.ascontiguousarray(np.concatenate([fk1, fv1], axis=1))

    def blockdiag(P):
        nh, r, dk = P.shape
        M = np.zeros((nh * r, nh * dk), f32)
        for h in range(nh):
            M[h * r:(h + 1) * r, h * dk:(h + 1) * dk] = P[h]
        return M

    pk0bd = blockdiag(pk0)
    pv0bd = blockdiag(pv0)
    pk1bd = blockdiag(pk1)
    pv1bd = blockdiag(pv1)

    gt = np.ascontiguousarray(gw.T)
    ut = np.ascontiguousarray(uw.T)
    dt = np.ascontiguousarray(dw.T)
    lg = np.ascontiguousarray(gt.reshape(16, 128, 64, 128).transpose(2, 1, 0, 3)
                              ).reshape(64, 128, 2048)
    lu = np.ascontiguousarray(ut.reshape(16, 128, 64, 128).transpose(2, 1, 0, 3)
                              ).reshape(64, 128, 2048)
    ld = np.ascontiguousarray(dt.reshape(64, 128, 16, 128).transpose(2, 1, 0, 3)
                              ).reshape(16, 128, 8192)
    gsc = np.ascontiguousarray(gs.reshape(64, 128).T)
    usc = np.ascontiguousarray(us.reshape(64, 128).T)
    dsc = np.ascontiguousarray(ds.reshape(16, 128).T)

    identb = np.eye(128, dtype=ml_dtypes.bfloat16)
    identf = np.eye(128, dtype=f32)
    ones_r = np.ones((1, 128), f32)

    masks = np.zeros((128, 4096), f32)
    qv = np.arange(512)
    for rt in range(4):
        kk = (rt * 128 + np.arange(128))[:, None]
        m = np.where(kk > qv[None, :], f32(-1e30), f32(0.0))
        masks[:, rt * 1024:rt * 1024 + 512] = m
        masks[:, rt * 1024 + 512:rt * 1024 + 1024] = m

    shared = dict(wq_f=wq, wk_f=wk, wv_f=wv, wo_f=wo, fkv0=fkv0, fkv1=fkv1,
                  pk0bd=pk0bd, pv0bd=pv0bd, pk1bd=pk1bd, pv1bd=pv1bd,
                  lg=lg, lu=lu, ld_t=ld, gsc_t=gsc, usc_t=usc, dsc_t=dsc,
                  identb=identb, identf=identf, ones_r=ones_r, masks=masks)

    in_maps = []
    qidx_all = []
    for c in range(8):
        seq = c // 4
        cp = c % 4
        xs = np.ascontiguousarray(x[seq])
        qpos = [512 * (4 * j + cp) for j in range(4)]
        qidx = np.concatenate([np.arange(q, q + 512) for q in qpos])
        xq = np.ascontiguousarray(xs[qidx])
        pad = np.zeros((NSLOT, 128, 64), f32)
        for j in range(4):
            nvalid = qpos[j] // 128
            pad[j, :, nvalid:] = -1e38
        m = dict(shared)
        m["x_full"] = xs
        m["x_q"] = xq
        m["padrow"] = pad
        in_maps.append(m)
        qidx_all.append((seq, qidx))
    return in_maps, qidx_all


def kernel(**inputs):
    in_maps, qidx_all = _host_prep(inputs)
    nc = _get_nc()
    res = run_bass_kernel_spmd(nc, in_maps, core_ids=list(range(8)))
    out = np.empty((2, SEQ, D), np.float32)
    for c in range(8):
        seq, qidx = qidx_all[c]
        out[seq, qidx] = np.asarray(res.results[c]["out_q"], np.float32)
    return out
